# version 06
# baseline (speedup 1.0000x reference)
import numpy as np
import ml_dtypes
import concourse.bass as bass
import concourse.mybir as mybir
from concourse.bass_utils import run_bass_kernel_spmd
from concourse.alu_op_type import AluOpType as OP
from contextlib import ExitStack

F32 = mybir.dt.float32
BF16 = mybir.dt.bfloat16
I32 = mybir.dt.int32
AF = mybir.ActivationFunctionType
BF = ml_dtypes.bfloat16

T = 2048
D = 1024
NCK = 8
H = 16
HD = 64
CL = 128
NCH = T // CL
SC = 512
NSC = T // SC
HID = 3072
ROWP = 4100
XW = 2050
INVD = 1.0 / 127.0

_PHASES = ("l0a", "l0f", "l1a", "l1f")
_ROUNDS = (0, 1)
_DUMP = None
_SKIP_LNFIN = False
_DBG_HSEG = False
_SERIAL = 0
_REVHC = 0
_DBG_LNST = False
_DBG_L1A = False


_UID = [0]


def _nm(b):
    _UID[0] += 1
    return f"{b}_{_UID[0]}"


class Ctr:
    def __init__(self, sem, dma=False):
        self.sem = sem
        self.n = 0
        self.dma = dma

    def inc(self, instr, k=None):
        if k is None:
            k = 16 if self.dma else 1
        instr.then_inc(self.sem, k)
        self.n += k

    def wait(self, eng, n=None):
        eng.wait_ge(self.sem, self.n if n is None else n)


def build_program(phases=None, rounds=None, dump=None):
    phases = _PHASES if phases is None else phases
    rounds = _ROUNDS if rounds is None else rounds
    nc = bass.Bass("TRN2", target_bir_lowering=False)

    din = {}

    def I(name, shape, dt=F32):
        din[name] = nc.dram_tensor(name, shape, dt, kind="ExternalInput").ap()
        return din[name]

    x_in = I("x", [2, D, T])
    I("conv_w", [3, D])
    VEC_NAMES = ("maa_x", "maa_w", "maa_k", "maa_v", "maa_r", "maa_g",
                 "time_decay", "ln_w", "ln_b", "gate_b",
                 "bq", "bk", "bv", "bf", "uflat", "vflat", "gamma", "beta",
                 "mu_r", "mu_k")
    for nm in VEC_NAMES:
        I(nm, [D])
    I("maa_w1", [D, 160])
    I("maa_w2", [5, 32, D])
    I("decay_w1", [D, 64])
    I("decay_w2", [64, D])
    for nm in ("Wr", "Wk", "Wv", "Wg", "Wo", "gate_w",
               "Wq", "Wkq", "Wvq", "Wpos", "Wf", "WrT2", "WkT2", "WvT2"):
        I(nm, [D, D])
    I("wgT", [D, HID], I32)
    I("wuT", [D, HID], I32)
    I("wdT", [HID, D], I32)
    I("peT", [D, T], BF16)
    I("ident", [128, 128], BF16)
    I("mask_fwd", [128, 128], BF16)
    I("mask_bwd", [128, 128], BF16)
    I("gnsel", [128, 16 * NCK], BF16)
    I("gnbc", [16, 128 * NCK], BF16)

    y_out = nc.dram_tensor("y", [2, D, T], F32, kind="ExternalOutput").ap()
    cc_dram = [nc.dram_tensor(f"ccd{i}", [128, ROWP], BF16, kind="Internal").ap()
               for i in range(2)]
    p_park = nc.dram_tensor("p_park", [D, T], BF16, kind="Internal").ap()
    lnstats = nc.dram_tensor("lnstats", [2, T], F32, kind="ExternalOutput" if _DBG_LNST else "Internal").ap()
    lndbg = nc.dram_tensor("lndbg", [4, T], F32, kind="ExternalOutput").ap() if _DBG_LNST else None

    with ExitStack() as ex:
        E = ex.enter_context

        xbuf = E(nc.sbuf_tensor(_nm("xbuf"), [128, NCK * XW], BF16))
        stage = E(nc.sbuf_tensor(_nm("stage"), [128, 1024], F32))
        stagei = stage[:].bitcast(I32)
        DVEC_NAMES = ("tap0", "tap1m1", "tap2", "gbh", "bqu", "bqv",
                      "murm1", "mukm1")
        ALLV = VEC_NAMES + DVEC_NAMES
        vbuf = E(nc.sbuf_tensor(_nm("vbuf"), [128, len(ALLV) * NCK], F32))
        _vslot = {nm: i * NCK for i, nm in enumerate(ALLV)}

        class _VD:
            def __getitem__(self, nm):
                b = _vslot[nm]
                return vbuf[:, b:b + NCK]
        vec = dvec = _VD()
        ident = E(nc.sbuf_tensor(_nm("identsb"), [128, 128], BF16))
        maskf = E(nc.sbuf_tensor(_nm("maskf"), [128, 128], BF16))
        maskb = E(nc.sbuf_tensor(_nm("maskb"), [128, 128], BF16))
        gnsel = gnbc = None

        C = {nm: Ctr(E(nc.semaphore(nm)), dma=nm.startswith("d"))
             for nm in ("dL", "dG", "dO", "t", "v", "s", "g", "b")}
        blk = E(nc.Block())

        def bodies(tensor=None, vector=None, scalar=None, gpsimd=None, sync=None):
            if sync is not None:
                blk.sync(sync)
            if gpsimd is not None:
                blk.gpsimd(gpsimd)
            if scalar is not None:
                blk.scalar(scalar)
            if vector is not None:
                blk.vector(vector)
            if tensor is not None:
                blk.tensor(tensor)

        def barrier():
            target = C["b"].n + 5

            def mk(eng):
                eng.drain()
                C["b"].inc(eng.nop(), 1)
                eng.wait_ge(C["b"].sem, target)
            bodies(tensor=mk, vector=mk, scalar=mk, gpsimd=mk, sync=mk)
            C["b"].n = target

        def xcol(oc, t0):
            return oc * XW + 1 + t0

        ML = 1024

        def _start_sync(sync):
            for nm in VEC_NAMES:
                src = bass.AP(din[nm].tensor, offset=0, ap=[[1, 128], [128, NCK]])
                C["dL"].inc(sync.dma_start(vec[nm][:], src,
                                           allow_slow_non_contiguous=True))
            for j, nm in ((0, "tap0"), (1, None), (2, "tap2")):
                if nm:
                    src = bass.AP(din["conv_w"].tensor, offset=j * D,
                                  ap=[[1, 128], [128, NCK]])
                    C["dL"].inc(sync.dma_start(dvec[nm][:], src,
                                               allow_slow_non_contiguous=True))
            src = bass.AP(din["conv_w"].tensor, offset=D, ap=[[1, 128], [128, NCK]])
            C["dL"].inc(sync.dma_start(dvec["tap1m1"][:], src,
                                       allow_slow_non_contiguous=True))
            for sb, nm in ((ident, "ident"), (maskf, "mask_fwd"), (maskb, "mask_bwd")):
                C["dL"].inc(sync.dma_start(sb[:], din[nm]))

        def _start_vector(vector):
            C["dL"].wait(vector)
            vector.memset(xbuf[:], 0.0)
            vector.tensor_scalar(dvec["tap1m1"][:], dvec["tap1m1"][:], -1.0, None, OP.add)
            vector.tensor_scalar(dvec["gbh"][:], vec["gate_b"][:], 0.5, None, OP.mult)
            vector.tensor_tensor(dvec["bqu"][:], vec["bq"][:], vec["uflat"][:], OP.add)
            vector.tensor_tensor(dvec["bqv"][:], vec["bq"][:], vec["vflat"][:], OP.add)
            vector.tensor_scalar(dvec["murm1"][:], vec["mu_r"][:], -1.0, 1.0, OP.mult, OP.add)
            ins = vector.tensor_scalar(dvec["mukm1"][:], vec["mu_k"][:], -1.0, 1.0, OP.mult, OP.add)
            C["v"].inc(ins)

        bodies(sync=_start_sync, vector=_start_vector)
        barrier()

        for rnd in rounds:
            _load_x(nc, blk, bodies, C, xbuf, stage, x_in, rnd, xcol)
            barrier()
            for ph in phases:
                ctx = dict(nc=nc, E=E, blk=blk, bodies=bodies, barrier=barrier,
                           C=C, din=din, xbuf=xbuf, stage=stage, stagei=stagei,
                           vec=vec, dvec=dvec, ident=ident, maskf=maskf,
                           maskb=maskb, gnsel=gnsel, gnbc=gnbc, xcol=xcol,
                           rnd=rnd, cc_dram=cc_dram, p_park=p_park,
                           lnstats=lnstats, ydbg=y_out, lndbg=lndbg)
                PHASE_FNS[ph](ctx)
                barrier()
                if dump == (rnd, ph):
                    _dump_x(nc, blk, bodies, C, xbuf, stage, y_out, rnd, xcol)
                    barrier()
                    _finish(bodies, C)
                    return nc, din
            if not _DBG_HSEG:
                _dump_x(nc, blk, bodies, C, xbuf, stage, y_out, rnd, xcol)
            barrier()
        _finish(bodies, C)

    return nc, din


def _finish(bodies, C):
    def mk(sync):
        C["dO"].wait(sync)
    bodies(sync=mk)


def _load_x(nc, blk, bodies, C, xbuf, stage, x_in, rnd, xcol):
    base_dL = C["dL"].n
    vbase = C["v"].n
    HW = 1024
    steps = [(oc, hh) for oc in range(NCK) for hh in range(2)]

    def syncb(sync):
        for i, (oc, hh) in enumerate(steps):
            if i > 0:
                sync.wait_ge(C["v"].sem, vbase + i)
            C["dL"].inc(sync.dma_start(
                stage[:],
                x_in[rnd, oc * 128:(oc + 1) * 128, hh * HW:(hh + 1) * HW]))

    def vector(vector):
        for i, (oc, hh) in enumerate(steps):
            vector.wait_ge(C["dL"].sem, base_dL + 16 * (i + 1))
            ins = vector.tensor_copy(
                xbuf[:, xcol(oc, hh * HW):xcol(oc, (hh + 1) * HW)],
                stage[:])
            C["v"].inc(ins)

    bodies(sync=syncb, vector=vector)


def _dump_x(nc, blk, bodies, C, xbuf, stage, y_out, rnd, xcol):
    vbase = C["v"].n
    dbase = C["dO"].n
    HW = 1024
    steps = [(oc, hh) for oc in range(NCK) for hh in range(2)]

    def vector(vector):
        for i, (oc, hh) in enumerate(steps):
            if i > 0:
                vector.wait_ge(C["dO"].sem, dbase + 16 * i)
            ins = vector.tensor_copy(stage[:],
                                     xbuf[:, xcol(oc, hh * HW):xcol(oc, (hh + 1) * HW)])
            C["v"].inc(ins)

    def sync(sync):
        for i, (oc, hh) in enumerate(steps):
            sync.wait_ge(C["v"].sem, vbase + i + 1)
            C["dO"].inc(sync.dma_start(
                y_out[rnd, oc * 128:(oc + 1) * 128, hh * HW:(hh + 1) * HW],
                stage[:, 0:HW]))
        C["dO"].wait(sync)

    bodies(vector=vector, sync=sync)





def _pe_table():
    pos = np.arange(T, dtype=np.float32)
    inv = 1.0 / (10000.0 ** (np.arange(0.0, D, 2.0, dtype=np.float32) / D))
    ang = pos[:, None] * inv[None, :]
    enc = np.zeros((T, D), np.float32)
    enc[:, 0::2] = np.sin(ang)
    enc[:, 1::2] = np.cos(ang)
    return enc


def _host_consts():
    gnsel = np.zeros((128, 16 * NCK), np.float32)
    gnbc = np.zeros((16, 128 * NCK), np.float32)
    for oc in range(NCK):
        gnsel[0:64, oc * 16 + 2 * oc] = 1.0
        gnsel[64:128, oc * 16 + 2 * oc + 1] = 1.0
        gnbc[2 * oc, oc * 128 + 0:oc * 128 + 64] = 1.0
        gnbc[2 * oc + 1, oc * 128 + 64:oc * 128 + 128] = 1.0
    return dict(
        ident=np.eye(128, dtype=BF),
        mask_fwd=np.triu(np.ones((128, 128), np.float32)).astype(BF),
        mask_bwd=np.tril(np.ones((128, 128), np.float32), -1).astype(BF),
        gnsel=gnsel.astype(BF),
        gnbc=gnbc.astype(BF),
        peT=np.ascontiguousarray(_pe_table().T).astype(BF),
    )


def _prep_inputs(inp):
    g = {}
    g["conv_w"] = np.ascontiguousarray(inp["L0_attn_conv_w"][:, 0, :].T)
    ren = {
        "maa_x": "L0_attn_maa_x", "maa_w": "L0_attn_maa_w", "maa_k": "L0_attn_maa_k",
        "maa_v": "L0_attn_maa_v", "maa_r": "L0_attn_maa_r", "maa_g": "L0_attn_maa_g",
        "time_decay": "L0_attn_time_decay", "ln_w": "L0_attn_ln_w",
        "ln_b": "L0_attn_ln_b", "gate_b": "L0_attn_gate_b",
        "maa_w1": "L0_attn_maa_w1", "decay_w1": "L0_attn_decay_w1",
        "decay_w2": "L0_attn_decay_w2",
        "Wr": "L0_attn_Wr", "Wk": "L0_attn_Wk", "Wv": "L0_attn_Wv",
        "Wg": "L0_attn_Wg", "Wo": "L0_attn_Wo", "gate_w": "L0_attn_gate_w",
        "Wq": "L1_attn_Wq", "Wkq": "L1_attn_Wk", "Wvq": "L1_attn_Wv",
        "Wpos": "L1_attn_Wpos", "Wf": "L1_attn_Wf",
        "bq": "L1_attn_bq", "bk": "L1_attn_bk", "bv": "L1_attn_bv",
        "bf": "L1_attn_bf", "gamma": "L1_attn_gamma", "beta": "L1_attn_beta",
        "mu_r": "L1_ffn_mu_r", "mu_k": "L1_ffn_mu_k",
    }
    for k, v in ren.items():
        g[k] = np.ascontiguousarray(inp[v])
    g["maa_w2"] = np.ascontiguousarray(inp["L0_attn_maa_w2"])
    g["uflat"] = np.ascontiguousarray(inp["L1_attn_u"].reshape(-1))
    g["vflat"] = np.ascontiguousarray(inp["L1_attn_v"].reshape(-1))
    g["wgT"] = np.ascontiguousarray(inp["L0_ffn_w_gate"].T)
    g["wuT"] = np.ascontiguousarray(inp["L0_ffn_w_up"].T)
    g["wdT"] = np.ascontiguousarray(inp["L0_ffn_w_down"].T)
    g["WrT2"] = np.ascontiguousarray(inp["L1_ffn_W_r"].T)
    g["WkT2"] = np.ascontiguousarray(inp["L1_ffn_W_k"].T)
    g["WvT2"] = np.ascontiguousarray(inp["L1_ffn_W_v"].T)
    g.update(_host_consts())
    return g


_CACHE = {}


def kernel(**inp):
    key = "prog"
    if key not in _CACHE:
        _CACHE[key] = build_program()
    nc, _ = _CACHE[key]
    g = _prep_inputs(inp)
    x = inp["L0_attn_x"]
    in_maps = []
    for c in range(8):
        m = dict(g)
        shard = np.ascontiguousarray(x[2 * c:2 * c + 2].transpose(0, 2, 1))
        m["x"] = shard
        in_maps.append(m)
    res = run_bass_kernel_spmd(nc, in_maps, core_ids=list(range(8)))
    outs = []
    for c in range(8):
        yc = res.results[c]["y"]
        outs.append(yc.transpose(0, 2, 1))
    return np.ascontiguousarray(np.concatenate(outs, axis=0)).astype(np.float32)


def _l0f(ctx):
    nc, E, bodies, C = ctx["nc"], ctx["E"], ctx["bodies"], ctx["C"]
    din, xbuf, stage, stagei = ctx["din"], ctx["xbuf"], ctx["stage"], ctx["stagei"]
    xcol = ctx["xcol"]
    TG = 256
    NG = T // TG
    NHC = HID // 128
    eps = 1e-5

    with ExitStack() as px:
        P = px.enter_context
        lnslot = P(nc.sbuf_tensor(_nm("lnslot"), [1, 2 * TG], F32))
        onesc = P(nc.sbuf_tensor(_nm("onesc"), [128, 1], BF16))
        lnstats = ctx["lnstats"]
        wx = px.enter_context(ExitStack())
        P = wx.enter_context
        wg = P(nc.sbuf_tensor(_nm("wg_sb"), [128, NCK * HID], BF16))
        wu = P(nc.sbuf_tensor(_nm("wu_sb"), [128, NCK * HID], BF16))
        wd = P(nc.sbuf_tensor(_nm("wd_sb"), [128, NHC * D], BF16))
        hseg = P(nc.sbuf_tensor(_nm("hseg"), [128, NHC * TG], BF16))
        if _DBG_HSEG:
            dbg1 = P(nc.sbuf_tensor(_nm("dbg1"), [128, TG], F32))
            dbg2 = P(nc.sbuf_tensor(_nm("dbg2"), [128, TG], F32))
        th = [stage[:, i * 128:(i + 1) * 128].bitcast(BF16) for i in range(2)]
        sqs = [stage[:, 256 + i * 128:256 + (i + 1) * 128].bitcast(BF16) for i in range(2)]
        h1s = [stage[:, 512 + i * 256:512 + (i + 1) * 256] for i in range(2)]

        def cast_int(dram, dst, nblk, ncols, scale):
            base = C["dL"].n
            ab = C["v"].n
            steps = [(b, hh) for b in range(nblk) for hh in range(ncols // 1024)]

            def sync(sync):
                for i, (b, hh) in enumerate(steps):
                    sync.wait_ge(C["v"].sem, ab + i)
                    C["dL"].inc(sync.dma_start(
                        stagei, dram[b * 128:(b + 1) * 128,
                                     hh * 1024:(hh + 1) * 1024]))

            def vector(vector):
                for i, (b, hh) in enumerate(steps):
                    vector.wait_ge(C["dL"].sem, base + 16 * (i + 1))
                    ins = vector.tensor_scalar(
                        dst[:, b * ncols + hh * 1024:b * ncols + (hh + 1) * 1024],
                        stagei, scale, None, OP.mult)
                    C["v"].inc(ins)
            bodies(sync=sync, vector=vector)

        cast_int(din["wgT"], wg, NCK, HID, INVD)
        cast_int(din["wuT"], wu, NCK, HID, INVD)
        base = C["dL"].n
        ab = C["v"].n

        def syncd(sync):
            for b in range(NHC):
                sync.wait_ge(C["v"].sem, ab + b)
                C["dL"].inc(sync.dma_start(
                    stagei, din["wdT"][b * 128:(b + 1) * 128, :]))

        def vecd(vector):
            for b in range(NHC):
                vector.wait_ge(C["dL"].sem, base + 16 * (b + 1))
                ins = vector.tensor_scalar(wd[:, b * D:(b + 1) * D],
                                           stagei, 0.5 * INVD, None, OP.mult)
                C["v"].inc(ins)
        bodies(sync=syncd, vector=vecd)

        def prep(vector):
            C["v"].inc(vector.memset(onesc[:], 1.0))
        bodies(vector=prep)
        ctx["barrier"]()

        pg = [P(nc.psum_tensor(_nm(f"pg{i}"), [128, 512], F32))[:, 0:TG] for i in range(2)]
        pu = [P(nc.psum_tensor(_nm(f"pu{i}"), [128, 512], F32))[:, 0:TG] for i in range(2)]
        pd = [P(nc.psum_tensor(_nm(f"pd{i}"), [128, 512], F32))[:, 0:TG] for i in range(2)]
        psum_s = P(nc.psum_tensor(_nm("pssum"), [1, 512], F32))[:, 0:TG]
        psum_q = P(nc.psum_tensor(_nm("pssq"), [1, 512], F32))[:, 0:TG]

        for g in range(NG):
            t0 = g * TG

            t_after_hc = []
            s_after_hc = []
            v_after_hc = []
            tb = C["t"].n
            sb_ = C["s"].n
            vb = C["v"].n
            for hc in range(NHC):
                t_after_hc.append(tb + 16 * (hc + 1))
                s_after_hc.append(sb_ + (hc + 1))
                v_after_hc.append(vb + 2 * (hc + 1))

            def thc(tensor, t0=t0, vb=vb, g=g):
                hcs = list(reversed(range(NHC))) if _REVHC else list(range(NHC))
                for pos, hc in enumerate(hcs):
                    i = pos % 2
                    if _SERIAL and pos >= 1:
                        tensor.wait_ge(C["v"].sem, vb + 2 * pos)
                    elif pos >= 2:
                        tensor.wait_ge(C["v"].sem, vb + 2 * (pos - 1))
                    for kc in range(NCK):
                        ins = tensor.matmul(pg[i],
                                            wg[:, kc * HID + hc * 128:kc * HID + (hc + 1) * 128],
                                            xbuf[:, xcol(kc, t0):xcol(kc, t0 + TG)],
                                            start=(kc == 0), stop=(kc == NCK - 1))
                        C["t"].inc(ins)
                    for kc in range(NCK):
                        ins = tensor.matmul(pu[i],
                                            wu[:, kc * HID + hc * 128:kc * HID + (hc + 1) * 128],
                                            xbuf[:, xcol(kc, t0):xcol(kc, t0 + TG)],
                                            start=(kc == 0), stop=(kc == NCK - 1))
                        C["t"].inc(ins)

            def shc(scalar, t_after_hc=t_after_hc, vb=vb):
                hcs = list(reversed(range(NHC))) if _REVHC else list(range(NHC))
                for pos, hc in enumerate(hcs):
                    i = pos % 2
                    scalar.wait_ge(C["t"].sem,
                                   t_after_hc[pos] if _SERIAL else t_after_hc[pos] - 8)
                    if pos >= 2:
                        scalar.wait_ge(C["v"].sem, vb + 2 * (pos - 1))
                    ins = scalar.activation(th[i], pg[i], AF.Tanh, scale=0.5)
                    C["s"].inc(ins)

            def vhc(vector, t_after_hc=t_after_hc, s_after_hc=s_after_hc):
                hcs = list(reversed(range(NHC))) if _REVHC else list(range(NHC))
                for pos, hc in enumerate(hcs):
                    i = pos % 2
                    vector.wait_ge(C["s"].sem, s_after_hc[pos])
                    ins = vector.scalar_tensor_tensor(
                        h1s[i], th[i], 1.0, pg[i], OP.add, OP.mult)
                    C["v"].inc(ins)
                    vector.wait_ge(C["t"].sem, t_after_hc[pos])
                    if _DBG_HSEG and g == 0 and hc == 16:
                        vector.tensor_copy(dbg1[:], pg[i])
                        vector.tensor_copy(dbg2[:], pu[i])
                    ins = vector.tensor_tensor(
                        hseg[:, hc * TG:(hc + 1) * TG], h1s[i], pu[i], OP.mult)
                    C["v"].inc(ins)
            bodies(tensor=thc, scalar=shc, vector=vhc)
            hseg_ready = C["v"].n
            if _DBG_HSEG and g == 0:
                yo = ctx["ydbg"]

                def ddump(gp, hseg_ready=hseg_ready):
                    gp.wait_ge(C["v"].sem, hseg_ready)
                    for j in range(3):
                        C["dO"].inc(gp.dma_start(
                            yo[0, j * 128:(j + 1) * 128, :],
                            hseg[:, j * 2048:(j + 1) * 2048]))
                    for j in range(3):
                        C["dO"].inc(gp.dma_start(
                            yo[0, 384 + j * 128:384 + (j + 1) * 128, 0:1024],
                            wg[:, j * 1024:(j + 1) * 1024]))
                    C["dO"].inc(gp.dma_start(yo[1, 896:1024, 0:TG], dbg1[:]))
                    C["dO"].inc(gp.dma_start(yo[1, 768:896, 0:TG], dbg2[:]))
                    for j in range(3):
                        C["dO"].inc(gp.dma_start(
                            yo[1, j * 128:(j + 1) * 128, 0:1024],
                            wu[:, j * 1024:(j + 1) * 1024]))
                    for j in range(3):
                        C["dO"].inc(gp.dma_start(
                            yo[1, 384 + j * 128:384 + (j + 1) * 128, 0:1024],
                            wu[:, 16 * 1024 + j * 1024:16 * 1024 + (j + 1) * 1024] if False else wg[:, (16 + j) * 1024:(17 + j) * 1024]))
                bodies(gpsimd=ddump)

            tb2 = C["t"].n
            vb2 = C["v"].n
            sb2 = C["s"].n

            def tdn(tensor, t0=t0, hseg_ready=hseg_ready, vb2=vb2, sb2=sb2, g=g):
                tensor.wait_ge(C["v"].sem, hseg_ready)
                for oc in range(NCK):
                    i = oc % 2
                    if oc >= 2:
                        tensor.wait_ge(C["v"].sem, vb2 + (oc - 1))
                    for hc in range(NHC):
                        ins = tensor.matmul(pd[i],
                                            wd[:, hc * D + oc * 128:hc * D + (oc + 1) * 128],
                                            hseg[:, hc * TG:(hc + 1) * TG],
                                            start=(hc == 0), stop=(hc == NHC - 1))
                        C["t"].inc(ins)
                for oc in range(NCK):
                    tensor.wait_ge(C["v"].sem, vb2 + oc + 1)
                    ins = tensor.matmul(psum_s, onesc[:],
                                        xbuf[:, xcol(oc, t0):xcol(oc, t0 + TG)],
                                        start=(oc == 0), stop=(oc == NCK - 1))
                    C["t"].inc(ins)
                    tensor.wait_ge(C["s"].sem, sb2 + oc + 1)
                    ins = tensor.matmul(psum_q, onesc[:],
                                        sqs[oc % 2],
                                        start=(oc == 0), stop=(oc == NCK - 1))
                    C["t"].inc(ins)

            def vdn(vector, t0=t0, tb2=tb2, g=g, dl0=C["dL"].n):
                for oc in range(NCK):
                    i = oc % 2
                    vector.wait_ge(C["t"].sem, tb2 + NHC * (oc + 1))
                    ins = vector.tensor_tensor(
                        xbuf[:, xcol(oc, t0):xcol(oc, t0 + TG)],
                        xbuf[:, xcol(oc, t0):xcol(oc, t0 + TG)], pd[i], OP.add)
                    C["v"].inc(ins)
                vector.wait_ge(C["t"].sem, tb2 + NHC * NCK + 2 * NCK)
                if g > 0:
                    vector.wait_ge(C["dL"].sem, dl0)
                ins = vector.tensor_copy(lnslot[0:1, 0:TG], psum_s)
                C["v"].inc(ins)
                ins = vector.tensor_copy(lnslot[0:1, TG:2 * TG], psum_q)
                C["v"].inc(ins)

            def sdn(scalar, t0=t0, vb2=vb2, tb2=tb2, g=g):
                for oc in range(NCK):
                    scalar.wait_ge(C["v"].sem, vb2 + oc + 1)
                    if oc >= 2:
                        scalar.wait_ge(C["t"].sem,
                                       tb2 + NHC * NCK + 2 * (oc - 2) + 2)
                    ins = scalar.activation(sqs[oc % 2],
                                            xbuf[:, xcol(oc, t0):xcol(oc, t0 + TG)],
                                            AF.Square)
                    C["s"].inc(ins)
            vdone = C["v"].n + NCK + 2

            def dstat(sync, t0=t0, vdone=vdone):
                sync.wait_ge(C["v"].sem, vdone)
                C["dL"].inc(sync.dma_start(lnstats[0:1, t0:t0 + TG], lnslot[0:1, 0:TG]))
                C["dL"].inc(sync.dma_start(lnstats[1:2, t0:t0 + TG], lnslot[0:1, TG:2 * TG]))
            bodies(tensor=tdn, vector=vdn, scalar=sdn, sync=dstat)

        ctx["barrier"]()
        wx.close()
        if not _SKIP_LNFIN:
            _ln_finalize(ctx, P, eps, onesc)


def _ln_finalize(ctx, P, eps, onesc):
    nc, bodies, C, xbuf, xcol = ctx["nc"], ctx["bodies"], ctx["C"], ctx["xbuf"], ctx["xcol"]
    with ExitStack() as fx:
        Q = fx.enter_context
        lnst = Q(nc.sbuf_tensor(_nm("lnstf"), [1, 2 * T], F32, side="right"))
        mu = Q(nc.sbuf_tensor(_nm("lnmu"), [1, T], F32, side="right"))
        rsd = Q(nc.sbuf_tensor(_nm("lnrsd"), [1, T], F32, side="right"))
        mub = Q(nc.sbuf_tensor(_nm("lnmub"), [1, T], BF16, side="right"))
        rsdb = Q(nc.sbuf_tensor(_nm("lnrsdb"), [1, T], BF16, side="right"))
        onesr = Q(nc.sbuf_tensor(_nm("onesr"), [1, 128], BF16, side="right"))
        scr = Q(nc.sbuf_tensor(_nm("lnscr"), [128, 512], F32, side="right"))
        pb0 = Q(nc.psum_tensor(_nm("pbm"), [128, 512], F32))
        pb1 = Q(nc.psum_tensor(_nm("pbr"), [128, 512], F32))

        def ldst(sync):
            C["dL"].wait(sync)
            C["dL"].inc(sync.dma_start(lnst[0:1, 0:T], ctx["lnstats"][0:1, :]))
            C["dL"].inc(sync.dma_start(lnst[0:1, T:2 * T], ctx["lnstats"][1:2, :]))
        bodies(sync=ldst)

        def vf2(vector):
            C["dL"].wait(vector)
            vector.memset(onesr[:], 1.0)
            vector.tensor_scalar(mu[:], lnst[0:1, 0:T], 1.0 / D, None, OP.mult)
            vector.tensor_scalar(rsd[:], lnst[0:1, T:2 * T], 1.0 / D, None, OP.mult)
            vector.tensor_tensor(lnst[0:1, 0:T], mu[:], mu[:], OP.mult)
            vector.tensor_tensor(rsd[:], rsd[:], lnst[0:1, 0:T], OP.subtract)
            ins = vector.tensor_scalar(rsd[:], rsd[:], eps, None, OP.add)
            C["v"].inc(ins)

        def sf(scalar):
            C["v"].wait(scalar)
            ins = scalar.activation(lnst[0:1, T:2 * T], rsd[:], AF.Sqrt)
            C["s"].inc(ins)

        def vf3(vector):
            C["s"].wait(vector)
            vector.reciprocal(rsd[:], lnst[0:1, T:2 * T])
            vector.tensor_copy(mub[:], mu[:])
            ins = vector.tensor_copy(rsdb[:], rsd[:])
            C["v"].inc(ins)
        bodies(vector=vf2)
        bodies(scalar=sf)
        bodies(vector=vf3)

        if _DBG_LNST and ctx["lndbg"] is not None:
            lndbg = ctx["lndbg"]

            def dbgd(sync):
                C["v"].wait(sync)
                C["dO"].inc(sync.dma_start(lndbg[0:1, :], mu[:]))
                C["dO"].inc(sync.dma_start(lndbg[1:2, :], rsd[:]))
                C["dO"].inc(sync.dma_start(lndbg[2:3, :], lnst[0:1, 0:T]))
                C["dO"].inc(sync.dma_start(lndbg[3:4, :], lnst[0:1, T:2 * T]))
            bodies(sync=dbgd)
        vb = C["v"].n
        tb = C["t"].n
        for ch in range(T // 512):
            t0 = ch * 512

            def tbc(tensor, t0=t0, ch=ch, vb=vb):
                if ch > 0:
                    tensor.wait_ge(C["v"].sem, vb + 2 * NCK * ch)
                tensor.wait_ge(C["v"].sem, vb)
                ins = tensor.matmul(pb0[:], onesr[:], mub[0:1, t0:t0 + 512],
                                    start=True, stop=True)
                C["t"].inc(ins)
                ins = tensor.matmul(pb1[:], onesr[:], rsdb[0:1, t0:t0 + 512],
                                    start=True, stop=True)
                C["t"].inc(ins)

            def vbc(vector, t0=t0, ch=ch, tb=tb):
                vector.wait_ge(C["t"].sem, tb + 2 * (ch + 1))
                for oc in range(NCK):
                    ins = vector.tensor_tensor(
                        scr[:], xbuf[:, xcol(oc, t0):xcol(oc, t0 + 512)],
                        pb0[:], OP.subtract)
                    C["v"].inc(ins)
                    ins = vector.tensor_tensor(
                        xbuf[:, xcol(oc, t0):xcol(oc, t0 + 512)],
                        scr[:], pb1[:], OP.mult)
                    C["v"].inc(ins)
            bodies(tensor=tbc, vector=vbc)


def _l1f(ctx):
    nc, bodies, C = ctx["nc"], ctx["bodies"], ctx["C"]
    din, xbuf, stage = ctx["din"], ctx["xbuf"], ctx["stage"]
    vec, dvec, xcol = ctx["vec"], ctx["dvec"], ctx["xcol"]
    TG = 512
    NG = T // TG
    eps = 1e-5

    with ExitStack() as px:
        P = px.enter_context
        wr2 = P(nc.sbuf_tensor(_nm("wr2"), [128, NCK * D], BF16))
        wk2 = P(nc.sbuf_tensor(_nm("wk2"), [128, NCK * D], BF16))
        wv2 = P(nc.sbuf_tensor(_nm("wv2"), [128, NCK * D], BF16))
        obuf = P(nc.sbuf_tensor(_nm("obuf"), [128, NCK * T], BF16))
        xr = P(nc.sbuf_tensor(_nm("xr2"), [128, NCK * TG], BF16))
        xk = P(nc.sbuf_tensor(_nm("xk2"), [128, NCK * TG], BF16))
        rb = P(nc.sbuf_tensor(_nm("rb2"), [128, NCK * TG], BF16))
        ka = P(nc.sbuf_tensor(_nm("ka2"), [128, NCK * TG], BF16))
        scr = P(nc.sbuf_tensor(_nm("scr2"), [128, TG], F32))
        sqs = P(nc.sbuf_tensor(_nm("sq2"), [128, 2 * TG], BF16))
        lnslot = P(nc.sbuf_tensor(_nm("lnslot2"), [1, 2 * TG], F32))
        onesc = P(nc.sbuf_tensor(_nm("onesc2"), [128, 1], BF16))
        lnstats = ctx["lnstats"]

        _cast_dd(ctx, din["WrT2"], wr2)
        _cast_dd(ctx, din["WkT2"], wk2)
        _cast_dd(ctx, din["WvT2"], wv2)

        def prep(vector):
            C["v"].inc(vector.memset(onesc[:], 1.0))
        bodies(vector=prep)
        ctx["barrier"]()

        pr = [P(nc.psum_tensor(_nm(f"pr2{i}"), [128, TG], F32)) for i in range(2)]
        po = [P(nc.psum_tensor(_nm(f"po2{i}"), [128, TG], F32)) for i in range(2)]
        pss = P(nc.psum_tensor(_nm("pss2"), [1, TG], F32))
        psq = P(nc.psum_tensor(_nm("psq2"), [1, TG], F32))

        for g in range(NG):
            t0 = g * TG

            def vmix(vector, t0=t0):
                for oc in range(NCK):
                    ins = vector.tensor_scalar(scr[:],
                                               xbuf[:, xcol(oc, t0):xcol(oc, t0 + TG)],
                                               vec["mu_r"][:, oc:oc + 1], None, OP.mult)
                    C["v"].inc(ins)
                    ins = vector.scalar_tensor_tensor(
                        xr[:, oc * TG:(oc + 1) * TG],
                        xbuf[:, xcol(oc, t0) - 1:xcol(oc, t0 + TG) - 1],
                        dvec["murm1"][:, oc:oc + 1], scr[:], OP.mult, OP.add)
                    C["v"].inc(ins)
                    ins = vector.tensor_scalar(scr[:],
                                               xbuf[:, xcol(oc, t0):xcol(oc, t0 + TG)],
                                               vec["mu_k"][:, oc:oc + 1], None, OP.mult)
                    C["v"].inc(ins)
                    ins = vector.scalar_tensor_tensor(
                        xk[:, oc * TG:(oc + 1) * TG],
                        xbuf[:, xcol(oc, t0) - 1:xcol(oc, t0 + TG) - 1],
                        dvec["mukm1"][:, oc:oc + 1], scr[:], OP.mult, OP.add)
                    C["v"].inc(ins)
            bodies(vector=vmix)
            mix_ready = C["v"].n

            tb = C["t"].n
            sb_ = C["s"].n

            def tproj(tensor, mix_ready=mix_ready, sb_=sb_):
                tensor.wait_ge(C["v"].sem, mix_ready)
                for oc in range(NCK):
                    i = oc % 2
                    if oc >= 2:
                        tensor.wait_ge(C["s"].sem, sb_ + 2 * (oc - 1))
                    for kc in range(NCK):
                        ins = tensor.matmul(pr[i][:],
                                            wr2[:, kc * D + oc * 128:kc * D + (oc + 1) * 128],
                                            xr[:, kc * TG:(kc + 1) * TG],
                                            start=(kc == 0), stop=(kc == NCK - 1))
                        C["t"].inc(ins)
                    for kc in range(NCK):
                        ins = tensor.matmul(po[i][:],
                                            wk2[:, kc * D + oc * 128:kc * D + (oc + 1) * 128],
                                            xk[:, kc * TG:(kc + 1) * TG],
                                            start=(kc == 0), stop=(kc == NCK - 1))
                        C["t"].inc(ins)

            def sproj(scalar, tb=tb):
                for oc in range(NCK):
                    i = oc % 2
                    scalar.wait_ge(C["t"].sem, tb + 16 * oc + 8)
                    ins = scalar.activation(rb[:, oc * TG:(oc + 1) * TG], pr[i][:],
                                            AF.Sigmoid)
                    C["s"].inc(ins)
                    scalar.wait_ge(C["t"].sem, tb + 16 * (oc + 1))
                    ins = scalar.activation(ka[:, oc * TG:(oc + 1) * TG], po[i][:],
                                            AF.Relu)
                    C["s"].inc(ins)
                for oc in range(NCK):
                    ins = scalar.activation(ka[:, oc * TG:(oc + 1) * TG],
                                            ka[:, oc * TG:(oc + 1) * TG], AF.Square)
                    C["s"].inc(ins)
            bodies(tensor=tproj, scalar=sproj)
            ka_ready = C["s"].n

            tb2 = C["t"].n
            vb2 = C["v"].n

            def tvo(tensor, ka_ready=ka_ready, vb2=vb2, t0=t0):
                tensor.wait_ge(C["s"].sem, ka_ready)
                for oc in range(NCK):
                    i = oc % 2
                    if oc >= 2:
                        tensor.wait_ge(C["v"].sem, vb2 + (oc - 1))
                    for kc in range(NCK):
                        ins = tensor.matmul(po[i][:],
                                            wv2[:, kc * D + oc * 128:kc * D + (oc + 1) * 128],
                                            ka[:, kc * TG:(kc + 1) * TG],
                                            start=(kc == 0), stop=(kc == NCK - 1))
                        C["t"].inc(ins)

            def vvo(vector, tb2=tb2, t0=t0):
                for oc in range(NCK):
                    i = oc % 2
                    vector.wait_ge(C["t"].sem, tb2 + 8 * (oc + 1))
                    ins = vector.tensor_tensor(
                        obuf[:, oc * T + t0:oc * T + t0 + TG],
                        rb[:, oc * TG:(oc + 1) * TG], po[i][:], OP.mult)
                    C["v"].inc(ins)
            bodies(tensor=tvo, vector=vvo)

        ctx["barrier"]()

        for g in range(NG):
            t0 = g * TG
            vb3 = C["v"].n
            sb3 = C["s"].n
            tb3 = C["t"].n

            def vres(vector, t0=t0):
                for oc in range(NCK):
                    ins = vector.tensor_tensor(
                        xbuf[:, xcol(oc, t0):xcol(oc, t0 + TG)],
                        xbuf[:, xcol(oc, t0):xcol(oc, t0 + TG)],
                        obuf[:, oc * T + t0:oc * T + t0 + TG], OP.add)
                    C["v"].inc(ins)

            def sres(scalar, t0=t0, vb3=vb3, tb3=tb3):
                for oc in range(NCK):
                    scalar.wait_ge(C["v"].sem, vb3 + oc + 1)
                    if oc >= 2:
                        scalar.wait_ge(C["t"].sem, tb3 + 2 * (oc - 1))
                    ins = scalar.activation(sqs[:, (oc % 2) * TG:(oc % 2 + 1) * TG],
                                            xbuf[:, xcol(oc, t0):xcol(oc, t0 + TG)],
                                            AF.Square)
                    C["s"].inc(ins)

            def tres(tensor, t0=t0, vb3=vb3, sb3=sb3, g=g):
                for oc in range(NCK):
                    tensor.wait_ge(C["v"].sem, vb3 + oc + 1)
                    ins = tensor.matmul(pss[:], onesc[:],
                                        xbuf[:, xcol(oc, t0):xcol(oc, t0 + TG)],
                                        start=(oc == 0), stop=(oc == NCK - 1))
                    C["t"].inc(ins)
                    tensor.wait_ge(C["s"].sem, sb3 + oc + 1)
                    ins = tensor.matmul(psq[:], onesc[:],
                                        sqs[:, (oc % 2) * TG:(oc % 2 + 1) * TG],
                                        start=(oc == 0), stop=(oc == NCK - 1))
                    C["t"].inc(ins)

            def vst(vector, t0=t0, g=g, dl0=C["dL"].n):
                C["t"].wait(vector)
                if g > 0:
                    vector.wait_ge(C["dL"].sem, dl0)
                ins = vector.tensor_copy(lnslot[0:1, 0:TG], pss[:])
                C["v"].inc(ins)
                ins = vector.tensor_copy(lnslot[0:1, TG:2 * TG], psq[:])
                C["v"].inc(ins)
            bodies(vector=vres, scalar=sres, tensor=tres)
            bodies(vector=vst)
            vdone2 = C["v"].n

            def dstat2(sync, t0=t0, vdone2=vdone2):
                sync.wait_ge(C["v"].sem, vdone2)
                C["dL"].inc(sync.dma_start(lnstats[0:1, t0:t0 + TG], lnslot[0:1, 0:TG]))
                C["dL"].inc(sync.dma_start(lnstats[1:2, t0:t0 + TG], lnslot[0:1, TG:2 * TG]))
            bodies(sync=dstat2)

        ctx["barrier"]()
        _ln_finalize(ctx, P, eps, onesc)


def _cast_dd(ctx, dram, dst, scale=1.0):
    nc, bodies, C, stage = ctx["nc"], ctx["bodies"], ctx["C"], ctx["stage"]
    base = C["dL"].n
    ab = C["v"].n

    def sync(sync):
        for kc in range(NCK):
            sync.wait_ge(C["v"].sem, ab + kc)
            C["dL"].inc(sync.dma_start(
                stage[:], dram[kc * 128:(kc + 1) * 128, :]))

    def vector(vector):
        for kc in range(NCK):
            vector.wait_ge(C["dL"].sem, base + 16 * (kc + 1))
            ins = vector.tensor_scalar(dst[:, kc * D:(kc + 1) * D],
                                       stage[:], scale, None, OP.mult)
            C["v"].inc(ins)
    bodies(sync=sync, vector=vector)


def _l0a(ctx):
    nc, E, bodies, C = ctx["nc"], ctx["E"], ctx["bodies"], ctx["C"]
    din, xbuf, stage = ctx["din"], ctx["xbuf"], ctx["stage"]
    vec, dvec, xcol = ctx["vec"], ctx["dvec"], ctx["xcol"]
    ident, maskf, maskb = ctx["ident"], ctx["maskf"], ctx["maskb"]
    barrier = ctx["barrier"]
    eps_gn = 1e-5 * 64.0
    eps_ln = 1e-5

    with ExitStack() as px:
        P = px.enter_context
        ybuf = P(nc.sbuf_tensor(_nm("ybuf"), [128, NCK * T], BF16))
        t5f = P(nc.sbuf_tensor(_nm("t5f"), [32, 4 * SC], BF16))
        lnslot = P(nc.sbuf_tensor(_nm("lnslot0"), [1, 2 * SC], F32))
        onesc = P(nc.sbuf_tensor(_nm("onesc0"), [128, 1], BF16))
        ones128 = P(nc.sbuf_tensor(_nm("ones128"), [128, 128], F32))
        bstash = P(nc.sbuf_tensor(_nm("bstash"), [128, NCK], BF16))
        lnstats = ctx["lnstats"]

        wab = px.enter_context(ExitStack())
        W = wab.enter_context
        w1 = P(nc.sbuf_tensor(_nm("w1_sb"), [128, NCK * 160], BF16))
        w2 = P(nc.sbuf_tensor(_nm("w2_sb"), [32, 5 * D], BF16))
        rkvs = px.enter_context(ExitStack())
        RK = rkvs.enter_context
        rbf = RK(nc.sbuf_tensor(_nm("rbf0"), [128, NCK * SC], BF16, side="right"))
        kbf = RK(nc.sbuf_tensor(_nm("kbf0"), [128, NCK * SC], BF16, side="right"))
        vT = RK(nc.sbuf_tensor(_nm("vT0"), [128, 4 * D], BF16, side="right"))
        wr = W(nc.sbuf_tensor(_nm("wr_sb"), [128, NCK * D], BF16, side="right"))
        wk = W(nc.sbuf_tensor(_nm("wk_sb"), [128, NCK * D], BF16, side="right"))
        wv = W(nc.sbuf_tensor(_nm("wv_sb"), [128, NCK * D], BF16, side="right"))
        dw1 = W(nc.sbuf_tensor(_nm("dw1_sb"), [128, NCK * 64], BF16, side="right"))
        dw2 = W(nc.sbuf_tensor(_nm("dw2_sb"), [64, D], BF16, side="right"))
        _cast_dd(ctx, din["Wr"], wr)
        _cast_dd(ctx, din["Wk"], wk)
        _cast_dd(ctx, din["Wv"], wv)
        _cast_blocks(ctx, din["maa_w1"], w1, NCK, 160, 128)
        _cast_blocks(ctx, din["maa_w2"].rearrange("f c d -> (f c) d"), w2, 5, D, 32)
        _cast_blocks(ctx, din["decay_w1"], dw1, NCK, 64, 128)
        _cast_blocks(ctx, din["decay_w2"], dw2, 1, D, 64)

        def prep(vector):
            vector.memset(ones128[:], 1.0)
            vector.memset(onesc[:], 1.0)
            C["v"].inc(vector.memset(ybuf[:], 0.0))
        bodies(vector=prep)
        barrier()

        scb = px.enter_context(ExitStack())
        S_ = scb.enter_context
        mix = S_(nc.sbuf_tensor(_nm("mix0"), [128, NCK * SC], BF16))
        dxs = S_(nc.sbuf_tensor(_nm("dxs0"), [128, NCK * SC], BF16))
        ew = S_(nc.sbuf_tensor(_nm("ew0"), [128, NCK * SC], F32))
        xxs = ew[:, 0:NCK * SC // 2].bitcast(BF16)
        fscr = ctx["stage"][:, 0:512]
        Ab = [S_(nc.sbuf_tensor(_nm(f"Ab{i}"), [128, 128], F32)) for i in range(2)]
        Nb = [S_(nc.sbuf_tensor(_nm(f"Nb{i}"), [128, 128], F32)) for i in range(2)]
        eA = [S_(nc.sbuf_tensor(_nm(f"eA{i}"), [128, 128], BF16)) for i in range(2)]
        eAm = [S_(nc.sbuf_tensor(_nm(f"eAm{i}"), [128, 128], BF16)) for i in range(2)]
        eMid = [S_(nc.sbuf_tensor(_nm(f"eMid{i}"), [128, 1], F32)) for i in range(2)]
        eL = [S_(nc.sbuf_tensor(_nm(f"eL{i}"), [128, 1], F32)) for i in range(2)]
        rt = [S_(nc.sbuf_tensor(_nm(f"rt{i}"), [128, 128], BF16)) for i in range(2)]
        kt = [S_(nc.sbuf_tensor(_nm(f"kt{i}"), [128, 128], BF16)) for i in range(2)]
        scbA = [S_(nc.sbuf_tensor(_nm(f"scbA{i}"), [128, 128], BF16)) for i in range(2)]
        scbB = [S_(nc.sbuf_tensor(_nm(f"scbB{i}"), [128, 128], BF16)) for i in range(2)]
        ktT = [S_(nc.sbuf_tensor(_nm(f"ktT{i}"), [128, 128], BF16)) for i in range(2)]
        s0f = [ctx["stage"][:, 768 + i * 64:768 + (i + 1) * 64] for i in range(2)]
        s0b = [S_(nc.sbuf_tensor(_nm(f"s0b{i}"), [128, 64], BF16)) for i in range(2)]
        sscr = [ctx["stage"][:, 896 + i * 64:896 + (i + 1) * 64] for i in range(2)]
        S0 = S_(nc.sbuf_tensor(_nm("S0buf"), [128, NCK * 64], F32))

        for pas in ("A", "B"):
            def z0(vector):
                C["v"].inc(vector.memset(S0[:], 0.0))
            bodies(vector=z0)
            barrier()
            scs = range(NSC) if pas == "A" else range(NSC - 1, -1, -1)
            for sc in scs:
                t0 = sc * SC
                _l0a_stage(ctx, locals())
                barrier()
                with ExitStack() as sps:
                    Q2 = sps.enter_context
                    sp = dict(
                        psc0=Q2(nc.psum_tensor(_nm("psc0"), [128, 512], F32)),
                        psc1=Q2(nc.psum_tensor(_nm("psc1"), [128, 512], F32)),
                        pkt=Q2(nc.psum_tensor(_nm("pkt"), [128, 512], F32)),
                        py=Q2(nc.psum_tensor(_nm("pys"), [128, 512], F32)),
                        psS=Q2(nc.psum_tensor(_nm("psSs"), [128, 512], F32)),
                        step=0, vb0=C["v"].n, sb0=C["s"].n, tb0=C["t"].n,
                        s1defer=[])
                    chs = range(4) if pas == "A" else range(3, -1, -1)
                    for cc_ in chs:
                        _l0a_scan(ctx, locals(), sp)
                    for _s1 in sp["s1defer"]:
                        bodies(scalar=_s1)
                barrier()
        wab.close()
        rkvs.close()
        wcd = px.enter_context(ExitStack())
        W = wcd.enter_context
        wg_ = W(nc.sbuf_tensor(_nm("wgg_sb"), [128, NCK * D], BF16, side="right"))
        wgate = W(nc.sbuf_tensor(_nm("wgate_sb"), [128, NCK * D], BF16, side="right"))
        wo = W(nc.sbuf_tensor(_nm("wo_sb"), [128, NCK * D], BF16, side="right"))
        gnselb = W(nc.sbuf_tensor(_nm("gnselb"), [128, 16 * NCK], BF16, side="right"))
        gnbcb = W(nc.sbuf_tensor(_nm("gnbcb"), [16, 128 * NCK], BF16, side="right"))
        _cast_dd(ctx, din["Wg"], wg_, scale=0.5)
        _cast_dd(ctx, din["gate_w"], wgate, scale=0.5)
        _cast_dd(ctx, din["Wo"], wo)

        def prepc(sync):
            C["dL"].inc(sync.dma_start(gnselb[:], din["gnsel"]))
            C["dL"].inc(sync.dma_start(gnbcb[:], din["gnbc"]))
        bodies(sync=prepc)
        dl_gn = C["dL"].n
        barrier()
        for sc in range(NSC):
            t0 = sc * SC
            _l0a_passc(ctx, locals())
            barrier()
        scb.close()
        wcd.close()
        _ln_finalize(ctx, P, eps_ln, onesc)


def _cast_blocks(ctx, dram, dst, nblk, ncols, nrows):
    nc, bodies, C, stage = ctx["nc"], ctx["bodies"], ctx["C"], ctx["stage"]
    ab = C["v"].n
    base = C["dL"].n

    def sync(sync):
        for b in range(nblk):
            sync.wait_ge(C["v"].sem, ab + b)
            C["dL"].inc(sync.dma_start(
                stage[0:nrows, 0:ncols], dram[b * nrows:(b + 1) * nrows, :]))

    def vector(vector):
        for b in range(nblk):
            vector.wait_ge(C["dL"].sem, base + 16 * (b + 1))
            ins = vector.tensor_copy(dst[0:nrows, b * ncols:(b + 1) * ncols],
                                     stage[0:nrows, 0:ncols])
            C["v"].inc(ins)
    bodies(sync=sync, vector=vector)


def _l1a(ctx):
    nc, E, bodies, C = ctx["nc"], ctx["E"], ctx["bodies"], ctx["C"]
    din, xbuf, stage = ctx["din"], ctx["xbuf"], ctx["stage"]
    vec, dvec, xcol, ident = ctx["vec"], ctx["dvec"], ctx["xcol"], ctx["ident"]
    barrier = ctx["barrier"]
    rnd, cc_dram, p_park = ctx["rnd"], ctx["cc_dram"], ctx["p_park"]
    lnstats = ctx["lnstats"]
    eps = 1e-5
    fscr = stage[:, 0:512]

    with ExitStack() as px:
        P = px.enter_context
        aof = P(nc.sbuf_tensor(_nm("aof"), [128, D], BF16))
        qvt = P(nc.sbuf_tensor(_nm("qvt"), [128, NCK * 129], BF16))
        lnslot = P(nc.sbuf_tensor(_nm("lnslotA"), [1, 256], F32))
        onesr = P(nc.sbuf_tensor(_nm("onesrA"), [1, 128], BF16))
        zcol = P(nc.sbuf_tensor(_nm("zcolA"), [128, 1], BF16))
        onescl = P(nc.sbuf_tensor(_nm("onesclA"), [128, 1], BF16))
        dvu = P(nc.sbuf_tensor(_nm("dvu"), [128, NCK], F32))
        negB = P(nc.sbuf_tensor(_nm("negB"), [128, 16], F32))
        Zb = P(nc.sbuf_tensor(_nm("Zb"), [128, 4], F32))
        zib = P(nc.sbuf_tensor(_nm("zib"), [128, 1], BF16))
        diagt = P(nc.sbuf_tensor(_nm("diagt"), [128, 128], BF16))
        bfrow = P(nc.sbuf_tensor(_nm("bfrow"), [1, D], BF16))
        bigs = px.enter_context(ExitStack())
        BG = bigs.enter_context
        qubuf = BG(nc.sbuf_tensor(_nm("qubuf"), [128, NCK * T], BF16))
        kbuf = BG(nc.sbuf_tensor(_nm("kbuf"), [128, NCK * T], BF16))
        vtbuf = BG(nc.sbuf_tensor(_nm("vtbuf"), [128, 16 * D], BF16))

        def prep(vector):
            vector.memset(onesr[:], 1.0)
            vector.memset(onescl[:], 1.0)
            vector.memset(zcol[:], 0.0)
            vector.memset(qvt[:], 0.0)
            ins = vector.tensor_tensor(dvu[:], vec["vflat"][:], vec["uflat"][:],
                                        OP.subtract)
            C["g"].inc(ins)
            vector.wait_ge(C["g"].sem, C["g"].n)
            ins = vector.tensor_scalar(dvu[:], dvu[:], 1.0 / 32.0, None, OP.mult)
            C["v"].inc(ins)
        bodies(vector=prep)

        db0 = C["dL"].n
        vb0 = C["v"].n

        def biasld(sync):
            for hh in range(2):
                sync.wait_ge(C["v"].sem, vb0 + hh)
                C["dL"].inc(sync.dma_start(
                    stage[0:1, 0:512],
                    bass.AP(din["bf"].tensor, offset=hh * 512, ap=[[512, 1], [1, 512]])))

        def biascast(vector):
            for hh in range(2):
                vector.wait_ge(C["dL"].sem, db0 + 16 * (hh + 1))
                ins = vector.tensor_copy(bfrow[0:1, hh * 512:(hh + 1) * 512],
                                         stage[0:1, 0:512])
                C["v"].inc(ins)
        bodies(sync=biasld, vector=biascast)
        barrier()

        def zdma(sync):
            for b in range(2):
                C["dL"].inc(sync.dma_start(cc_dram[b][:, 2048:2049], zcol[:],
                                           allow_slow_non_contiguous=True))
        bodies(sync=zdma)

        with ExitStack() as xs_:
            X = xs_.enter_context
            xn = X(nc.sbuf_tensor(_nm("xnA"), [128, NCK * T], BF16, side="right"))
            mub = X(nc.sbuf_tensor(_nm("mubA"), [1, 2 * 512], BF16, side="right"))
            bvrow = X(nc.sbuf_tensor(_nm("bvrow"), [1, D], BF16, side="right"))
            sqt = X(nc.sbuf_tensor(_nm("sqtA"), [128, 2 * 512], BF16, side="right"))
            stt_ = X(nc.sbuf_tensor(_nm("sttA"), [1, 3 * 512], F32, side="right"))

            dbv = C["dL"].n
            vbv = C["v"].n

            def bvld(sync):
                for hh in range(2):
                    sync.wait_ge(C["v"].sem, vbv + hh)
                    C["dL"].inc(sync.dma_start(
                        stage[0:1, 0:512],
                        bass.AP(din["bv"].tensor, offset=hh * 512,
                                ap=[[512, 1], [1, 512]])))

            def bvcast(vector):
                for hh in range(2):
                    vector.wait_ge(C["dL"].sem, dbv + 16 * (hh + 1))
                    ins = vector.tensor_copy(bvrow[0:1, hh * 512:(hh + 1) * 512],
                                             stage[0:1, 0:512])
                    C["v"].inc(ins)
            bodies(sync=bvld, vector=bvcast)

            with ExitStack() as ps:
                Q = ps.enter_context
                pss = Q(nc.psum_tensor(_nm("pssA"), [1, 512], F32))
                psq = Q(nc.psum_tensor(_nm("psqA"), [1, 512], F32))
                pmu = Q(nc.psum_tensor(_nm("pmuA"), [128, 512], F32))
                prs = Q(nc.psum_tensor(_nm("prsA"), [128, 512], F32))

                for gq in range(4):
                    t0 = gq * 512
                    sb1 = C["s"].n
                    tb1 = C["t"].n

                    def sq1(scalar, t0=t0, tb1=tb1):
                        for oc in range(NCK):
                            if oc >= 2:
                                scalar.wait_ge(C["t"].sem, tb1 + 2 * (oc - 1))
                            ins = scalar.activation(
                                sqt[:, (oc % 2) * 512:(oc % 2 + 1) * 512],
                                xbuf[:, xcol(oc, t0):xcol(oc, t0 + 512)], AF.Square)
                            C["s"].inc(ins)

                    def st1(tensor, t0=t0, sb1=sb1):
                        for oc in range(NCK):
                            ins = tensor.matmul(pss[:], onescl[:],
                                                xbuf[:, xcol(oc, t0):xcol(oc, t0 + 512)],
                                                start=(oc == 0), stop=(oc == NCK - 1))
                            C["t"].inc(ins)
                            tensor.wait_ge(C["s"].sem, sb1 + oc + 1)
                            ins = tensor.matmul(psq[:], onescl[:],
                                                sqt[:, (oc % 2) * 512:(oc % 2 + 1) * 512],
                                                start=(oc == 0), stop=(oc == NCK - 1))
                            C["t"].inc(ins)
                    bodies(scalar=sq1, tensor=st1)
                    stat_done = C["t"].n

                    def vst(vector):
                        vector.wait_ge(C["t"].sem, stat_done)
                        vector.tensor_scalar(stt_[0:1, 0:512], pss[:], 1.0 / D,
                                             None, OP.mult)
                        vector.tensor_scalar(stt_[0:1, 512:1024], psq[:], 1.0 / D,
                                             None, OP.mult)
                        vector.tensor_tensor(stt_[0:1, 1024:1536], stt_[0:1, 0:512],
                                             stt_[0:1, 0:512], OP.mult)
                        vector.tensor_tensor(stt_[0:1, 512:1024], stt_[0:1, 512:1024],
                                             stt_[0:1, 1024:1536], OP.subtract)
                        ins = vector.tensor_scalar(stt_[0:1, 512:1024],
                                                   stt_[0:1, 512:1024], eps, None, OP.add)
                        C["v"].inc(ins)
                    bodies(vector=vst)

                    def sst(scalar):
                        C["v"].wait(scalar)
                        ins = scalar.activation(stt_[0:1, 1024:1536],
                                                stt_[0:1, 512:1024], AF.Sqrt)
                        C["s"].inc(ins)
                    bodies(scalar=sst)

                    def vst2(vector):
                        C["s"].wait(vector)
                        vector.reciprocal(stt_[0:1, 512:1024], stt_[0:1, 1024:1536])
                        vector.tensor_copy(mub[0:1, 0:512], stt_[0:1, 0:512])
                        ins = vector.tensor_copy(mub[0:1, 512:1024],
                                                 stt_[0:1, 512:1024])
                        C["v"].inc(ins)
                    bodies(vector=vst2)
                    stats_v = C["v"].n
                    tb2 = C["t"].n

                    def tbc(tensor):
                        tensor.wait_ge(C["v"].sem, stats_v)
                        ins = tensor.matmul(pmu[:], onesr[:], mub[0:1, 0:512],
                                            start=True, stop=True)
                        C["t"].inc(ins)
                        ins = tensor.matmul(prs[:], onesr[:], mub[0:1, 512:1024],
                                            start=True, stop=True)
                        C["t"].inc(ins)

                    def vxn(vector, t0=t0, tb2=tb2):
                        vector.wait_ge(C["t"].sem, tb2 + 2)
                        for oc in range(NCK):
                            vector.tensor_tensor(
                                fscr[:], xbuf[:, xcol(oc, t0):xcol(oc, t0 + 512)],
                                pmu[:], OP.subtract)
                            vector.tensor_tensor(fscr[:], fscr[:], prs[:], OP.mult)
                            ins = vector.tensor_scalar(
                                xn[:, oc * T + t0:oc * T + t0 + 512], fscr[:],
                                vec["gamma"][:, oc:oc + 1], vec["beta"][:, oc:oc + 1],
                                OP.mult, OP.add)
                            C["v"].inc(ins)
                    bodies(tensor=tbc, vector=vxn)
            barrier()

            with ExitStack() as ps:
                Q = ps.enter_context
                pe_ = [Q(nc.psum_tensor(_nm(f"peA{i}"), [128, 512], F32))
                       for i in range(2)]
                for which in ("qu", "k", "vt"):
                    with ExitStack() as ws:
                        W_ = ws.enter_context(nc.sbuf_tensor(
                            _nm("wprj"), [128, NCK * D], BF16, side="right"))
                        srcnm = {"qu": "Wq", "k": "Wkq", "vt": "Wvq"}[which]
                        _cast_dd(ctx, din[srcnm], W_)
                        barrier()
                        for gq in range(4):
                            t0 = gq * 512
                            tb3 = C["t"].n
                            vb3 = C["v"].n
                            if which in ("qu", "k"):
                                dst = qubuf if which == "qu" else kbuf

                                def proj(tensor, t0=t0, W_=W_, vb3=vb3):
                                    for oc in range(NCK):
                                        if oc >= 2:
                                            tensor.wait_ge(C["v"].sem, vb3 + oc - 1)
                                        for kc in range(NCK):
                                            ins = tensor.matmul(
                                                pe_[oc % 2][:],
                                                W_[:, kc * D + oc * 128:kc * D + (oc + 1) * 128],
                                                xn[:, kc * T + t0:kc * T + t0 + 512],
                                                start=(kc == 0), stop=(kc == NCK - 1))
                                            C["t"].inc(ins)

                                def evac(vector, dst=dst, t0=t0, tb3=tb3, which=which):
                                    for oc in range(NCK):
                                        vector.wait_ge(C["t"].sem, tb3 + 8 * (oc + 1))
                                        if which == "qu":
                                            ins = vector.tensor_scalar(
                                                dst[:, oc * T + t0:oc * T + t0 + 512],
                                                pe_[oc % 2][:],
                                                dvec["bqu"][:, oc:oc + 1],
                                                1.0 / 32.0, OP.add, OP.mult)
                                        else:
                                            ins = vector.tensor_scalar(
                                                dst[:, oc * T + t0:oc * T + t0 + 512],
                                                pe_[oc % 2][:], vec["bk"][:, oc:oc + 1],
                                                None, OP.add)
                                        C["v"].inc(ins)
                                bodies(tensor=proj, vector=evac)
                            else:
                                def projv(tensor, t0=t0, W_=W_, vb3=vb3):
                                    for sub in range(4):
                                        for half in range(2):
                                            i = (sub * 2 + half) % 2
                                            if sub * 2 + half >= 2:
                                                tensor.wait_ge(C["v"].sem,
                                                               vb3 + sub * 2 + half - 1)
                                            for kc in range(NCK):
                                                ins = tensor.matmul(
                                                    pe_[i][:],
                                                    xn[:, kc * T + t0 + sub * 128:
                                                       kc * T + t0 + (sub + 1) * 128],
                                                    W_[:, kc * D + half * 512:
                                                       kc * D + (half + 1) * 512],
                                                    start=(kc == 0), stop=False)
                                                C["t"].inc(ins)
                                            ins = tensor.matmul(
                                                pe_[i][:], onesr[:],
                                                bvrow[0:1, half * 512:(half + 1) * 512],
                                                start=False, stop=True)
                                            C["t"].inc(ins)

                                def evacv(vector, t0=t0, tb3=tb3):
                                    for sub in range(4):
                                        for half in range(2):
                                            i = (sub * 2 + half) % 2
                                            vector.wait_ge(
                                                C["t"].sem,
                                                tb3 + 9 * (sub * 2 + half + 1))
                                            jc = (t0 + sub * 128) // 128
                                            ins = vector.tensor_copy(
                                                vtbuf[:, jc * D + half * 512:
                                                      jc * D + (half + 1) * 512],
                                                pe_[i][:])
                                            C["v"].inc(ins)
                                bodies(tensor=projv, vector=evacv)
                        barrier()
        barrier()

        pbuf = BG(nc.sbuf_tensor(_nm("pbuf"), [128, NCK * T], BF16))
        _l1a_pproj(ctx, px, locals())
        barrier()

        ccs = BG(nc.sbuf_tensor(_nm("ccs"), [128, 2304], BF16))
        winb = BG(nc.sbuf_tensor(_nm("winb"), [128, T], BF16))
        attnb = BG(nc.sbuf_tensor(_nm("attnb"), [128, T], BF16))
        attnT = BG(nc.sbuf_tensor(_nm("attnT"), [128, T], BF16))
        _l1a_bnorm(ctx, px, locals())
        barrier()
        _l1a_main(ctx, px, locals())
        barrier()
        bigs.close()
        _ln_finalize(ctx, P, eps, onescl)


def _l1a_pproj(ctx, px, g):
    nc, bodies, C = ctx["nc"], ctx["bodies"], ctx["C"]
    din, p_park, rnd = ctx["din"], ctx["p_park"], ctx["rnd"]
    pbuf = g["pbuf"]
    if rnd == 1:
        db = C["dL"].n

        def ld(sync):
            sync.wait_ge(C["dL"].sem, db)
            for oc in range(NCK):
                C["dL"].inc(sync.dma_start(
                    pbuf[:, oc * T:(oc + 1) * T],
                    p_park[oc * 128:(oc + 1) * 128, :]))

        def ack(vector, db=db):
            vector.wait_ge(C["dL"].sem, db + 16 * NCK)
            C["v"].inc(vector.memset(g["zcol"][:], 0.0))
        bodies(sync=ld, vector=ack)
        return

    with ExitStack() as wps:
        W = wps.enter_context
        wpos = W(nc.sbuf_tensor(_nm("wpos_sb"), [128, NCK * D], BF16, side="right"))
        peb = W(nc.sbuf_tensor(_nm("peb"), [128, NCK * 512], BF16, side="right"))
        _cast_dd(ctx, din["Wpos"], wpos)
        ctx["barrier"]()
        with ExitStack() as ps:
            Q = ps.enter_context
            pp = [Q(nc.psum_tensor(_nm(f"ppp{i}"), [128, 512], F32)) for i in range(2)]
            for gq in range(4):
                t0 = gq * 512
                db = C["dL"].n
                tb = C["t"].n
                vb = C["v"].n

                def peld(sync, t0=t0, vb=vb):
                    if t0 > 0:
                        sync.wait_ge(C["v"].sem, vb)
                    for kc in range(NCK):
                        C["dL"].inc(sync.dma_start(
                            peb[:, kc * 512:(kc + 1) * 512],
                            din["peT"][kc * 128:(kc + 1) * 128, t0:t0 + 512]))

                def pmm(tensor, db=db, vb=vb):
                    tensor.wait_ge(C["dL"].sem, db + 16 * NCK)
                    for oc in range(NCK):
                        if oc >= 2:
                            tensor.wait_ge(C["v"].sem, vb + oc - 1)
                        for kc in range(NCK):
                            ins = tensor.matmul(
                                pp[oc % 2][:],
                                wpos[:, kc * D + oc * 128:kc * D + (oc + 1) * 128],
                                peb[:, kc * 512:(kc + 1) * 512],
                                start=(kc == 0), stop=(kc == NCK - 1))
                            C["t"].inc(ins)

                def pev(vector, t0=t0, tb=tb):
                    for oc in range(NCK):
                        vector.wait_ge(C["t"].sem, tb + 8 * (oc + 1))
                        ins = vector.tensor_copy(
                            pbuf[:, oc * T + t0:oc * T + t0 + 512], pp[oc % 2][:])
                        C["v"].inc(ins)
                bodies(sync=peld, tensor=pmm, vector=pev)
        vdone = C["v"].n

        def park(sync):
            sync.wait_ge(C["v"].sem, vdone)
            for oc in range(NCK):
                C["dL"].inc(sync.dma_start(p_park[oc * 128:(oc + 1) * 128, :],
                                           pbuf[:, oc * T:(oc + 1) * T]))
        bodies(sync=park)


def _l1a_bnorm(ctx, px, g):
    nc, bodies, C, din = ctx["nc"], ctx["bodies"], ctx["C"], ctx["din"]
    ident = ctx["ident"]
    qubuf, kbuf, pbuf = g["qubuf"], g["kbuf"], g["pbuf"]
    dvu, negB, onesr = g["dvu"], g["negB"], g["onesr"]
    sqb = [g["attnb"], g["attnT"]]

    with ExitStack() as ps:
        Q = ps.enter_context
        gnsel2 = Q(nc.sbuf_tensor(_nm("gnsel2"), [128, 16 * NCK], BF16))
        nb = Q(nc.sbuf_tensor(_nm("nbuf"), [16, T], F32))
        mx = Q(nc.sbuf_tensor(_nm("mx"), [16, 8], F32))
        brow = Q(nc.sbuf_tensor(_nm("brow"), [1, 16], F32))
        browb = Q(nc.sbuf_tensor(_nm("browb"), [1, 16], BF16))
        b16 = Q(nc.sbuf_tensor(_nm("b16"), [16, 1], BF16))
        pn4 = [Q(nc.psum_tensor(_nm(f"pnB{i}"), [16, 512], F32)) for i in range(4)]
        pb2 = Q(nc.psum_tensor(_nm("pb2B"), [128, 512], F32))

        db = C["dL"].n

        def ld(sync):
            C["dL"].inc(sync.dma_start(gnsel2[:], din["gnsel"]))
        bodies(sync=ld)

        for idx, (buf, bias) in enumerate(((qubuf, None), (qubuf, dvu),
                                           (kbuf, None), (pbuf, None))):
            sb = C["s"].n
            tb = C["t"].n

            def sq(scalar, buf=buf, bias=bias, tb=tb):
                scalar.wait_ge(C["dL"].sem, db + 16)
                for oc in range(NCK):
                    if oc >= 2:
                        scalar.wait_ge(C["t"].sem, tb + 4 * (oc - 1))
                    if bias is None:
                        ins = scalar.activation(sqb[oc % 2][:],
                                                buf[:, oc * T:(oc + 1) * T], AF.Square)
                    else:
                        ins = scalar.activation(sqb[oc % 2][:],
                                                buf[:, oc * T:(oc + 1) * T], AF.Square,
                                                bias=bias[:, oc:oc + 1])
                    C["s"].inc(ins)

            def smm(tensor, sb=sb):
                for oc in range(NCK):
                    tensor.wait_ge(C["s"].sem, sb + oc + 1)
                    for ch in range(4):
                        ins = tensor.matmul(
                            pn4[ch][:], gnsel2[:, oc * 16:(oc + 1) * 16],
                            sqb[oc % 2][:, ch * 512:(ch + 1) * 512],
                            start=(oc == 0), stop=(oc == NCK - 1))
                        C["t"].inc(ins)
            bodies(scalar=sq, tensor=smm)
            mm_done = C["t"].n
            sb2 = C["s"].n

            def snorm(scalar, mm_done=mm_done):
                scalar.wait_ge(C["t"].sem, mm_done)
                for ch in range(4):
                    ins = scalar.activation(nb[0:16, ch * 512:(ch + 1) * 512],
                                            pn4[ch][:], AF.Sqrt)
                    C["s"].inc(ins)

            def vmax(vector, idx=idx, sb2=sb2):
                vector.wait_ge(C["s"].sem, sb2 + 4)
                ins = vector.reduce_max(mx[0:16, idx:idx + 1], nb[0:16, :],
                                        axis=mybir.AxisListType.X)
                C["v"].inc(ins)
            bodies(scalar=snorm)
            bodies(vector=vmax)
            ctx["barrier"]()

        if _DBG_L1A:
            yo = ctx["ydbg"]

            def dmx(gp):
                C["v"].wait(gp)
                C["dO"].inc(gp.dma_start(yo[1, 896:912, 0:8], mx[0:16, :]))
                C["dO"].inc(gp.dma_start(yo[1, 960:976, :], nb[0:16, :]))
            bodies(gpsimd=dmx)

        def vcomb(vector):
            ins = vector.tensor_tensor(mx[0:16, 4:5], mx[0:16, 0:1], mx[0:16, 2:3],
                                       OP.mult)
            C["g"].inc(ins)
            ins = vector.tensor_tensor(mx[0:16, 5:6], mx[0:16, 1:2], mx[0:16, 3:4],
                                       OP.mult)
            C["g"].inc(ins)
            vector.wait_ge(C["g"].sem, C["g"].n)
            ins = vector.tensor_tensor(mx[0:16, 4:5], mx[0:16, 4:5], mx[0:16, 5:6],
                                       OP.add)
            C["g"].inc(ins)
            vector.wait_ge(C["g"].sem, C["g"].n)
            ins = vector.tensor_copy(b16[:], mx[0:16, 4:5])
            C["v"].inc(ins)
        bodies(vector=vcomb)
        tb9 = C["t"].n

        def trow(tensor):
            C["v"].wait(tensor)
            ins = tensor.matmul(pb2[0:1, 0:16], b16[:], ident[0:16, 0:16],
                                start=True, stop=True)
            C["t"].inc(ins)
        bodies(tensor=trow)

        def vrow(vector):
            vector.wait_ge(C["t"].sem, tb9 + 1)
            ins = vector.tensor_copy(brow[:], pb2[0:1, 0:16])
            C["g"].inc(ins)
            vector.wait_ge(C["g"].sem, C["g"].n)
            ins = vector.tensor_copy(browb[:], brow[:])
            C["v"].inc(ins)
        bodies(vector=vrow)
        tb10 = C["t"].n

        def tbc2(tensor):
            C["v"].wait(tensor)
            ins = tensor.matmul(pb2[:, 0:16], onesr[:], browb[:], start=True, stop=True)
            C["t"].inc(ins)
        bodies(tensor=tbc2)

        def vneg(vector):
            vector.wait_ge(C["t"].sem, tb10 + 1)
            ins = vector.tensor_scalar(negB[:], pb2[:, 0:16], -1.0, None, OP.mult)
            C["v"].inc(ins)
        bodies(vector=vneg)
        if _DBG_L1A:
            yo2 = ctx["ydbg"]

            def dfin(gp):
                C["v"].wait(gp)
                C["dO"].inc(gp.dma_start(yo2[1, 912:913, 0:16], brow[:]))
                C["dO"].inc(gp.dma_start(yo2[1, 913:929, 0:8], mx[0:16, :]))
                C["dO"].inc(gp.dma_start(yo2[1, 929:945, 0:1], b16[:], allow_slow_non_contiguous=True))
            bodies(gpsimd=dfin)


def _l1a_main(ctx, px, g):
    nc, bodies, C, din = ctx["nc"], ctx["bodies"], ctx["C"], ctx["din"]
    ident, xcol, xbuf = ctx["ident"], ctx["xcol"], ctx["xbuf"]
    cc_dram, lnstats = ctx["cc_dram"], ctx["lnstats"]
    qubuf, kbuf, vtbuf, pbuf = g["qubuf"], g["kbuf"], g["vtbuf"], g["pbuf"]
    ccs, winb, attnb, attnT = g["ccs"], g["winb"], g["attnb"], g["attnT"]
    aof, qvt, dvu, negB = g["aof"], g["qvt"], g["dvu"], g["negB"]
    Zb, zib, diagt, onesr = g["Zb"], g["zib"], g["diagt"], g["onesr"]
    bfrow, lnslot, onescl = g["bfrow"], g["lnslot"], g["onescl"]
    fscr = ctx["stage"][:, 0:512]

    with ExitStack() as ms:
        W = ms.enter_context
        wf = W(nc.sbuf_tensor(_nm("wf_sb"), [128, NCK * D], BF16, side="right"))
        sqv = W(nc.sbuf_tensor(_nm("sqvA"), [128, 2 * 128], BF16))
        _cast_dd(ctx, din["Wf"], wf)
        ctx["barrier"]()

        Q = ms.enter_context
        ps_s = Q(nc.psum_tensor(_nm("ps_s"), [128, 1024], F32))
        ps_b = Q(nc.psum_tensor(_nm("ps_b"), [128, 512], F32))
        ps_T = Q(nc.psum_tensor(_nm("ps_T"), [128, 1024], F32))
        ps_pv = Q(nc.psum_tensor(_nm("ps_pv"), [128, 1024], F32))
        pstt = Q(nc.psum_tensor(_nm("pstt"), [1, 512], F32))

        for ti in range(16):
            i0 = ti * 128
            vbq = C["v"].n

            tprev = C["t"].n

            def vqv(vector, ti=ti, i0=i0, tprev=tprev):
                vector.wait_ge(C["t"].sem, tprev)
                for oc in range(NCK):
                    wdt = 129 if ti < 15 else 128
                    ins = vector.tensor_scalar(
                        qvt[:, oc * 129:oc * 129 + wdt],
                        qubuf[:, oc * T + i0:oc * T + i0 + wdt],
                        dvu[:, oc:oc + 1], None, OP.add)
                    C["v"].inc(ins)
            bodies(vector=vqv)
            qv_ready = C["v"].n

            s_base = C["s"].n
            sex_defer = []
            for h in range(16):
                th = ti * 16 + h
                hp = h // 2
                hr = (h % 2) * 64
                c0A = max(0, 1920 - i0)
                wB = 4095 - i0 - 2049
                achunks = []
                c = c0A
                while c < 2048:
                    w = min(512, 2048 - c)
                    achunks.append((c, w))
                    c += w
                bchunks = []
                c = 2049
                while c < 2049 + wB:
                    w = min(512, 2049 + wB - c)
                    bchunks.append((c, w))
                    c += w

                tb = C["t"].n
                vb = C["v"].n
                sb = C["s"].n
                db = C["dL"].n
                nA, nB = len(achunks), len(bchunks)

                def tbd(tensor, ti=ti, i0=i0, h=h, hp=hp, hr=hr,
                        achunks=achunks, bchunks=bchunks, vb=vb, sb=sb,
                        qv_ready=qv_ready):
                    tensor.wait_ge(C["v"].sem, qv_ready)
                    for j, (c, w) in enumerate(achunks):
                        if j >= 1:
                            tensor.wait_ge(C["v"].sem, vb + j)
                        ins = tensor.matmul(
                            ps_b[:, 0:w],
                            qvt[hr:hr + 64, hp * 129:hp * 129 + 128],
                            pbuf[hr:hr + 64, hp * T + c:hp * T + c + w],
                            start=True, stop=True)
                        C["t"].inc(ins)
                    nA_ = len(achunks)
                    for j, (c, w) in enumerate(bchunks):
                        tensor.wait_ge(C["v"].sem, vb + nA_ + j)
                        ins = tensor.matmul(
                            ps_b[:, 0:w],
                            qvt[hr:hr + 64, hp * 129 + 1:hp * 129 + 129],
                            pbuf[hr:hr + 64, hp * T + c - 2049:hp * T + c - 2049 + w],
                            start=True, stop=True)
                        C["t"].inc(ins)

                def vbd(vector, achunks=achunks, bchunks=bchunks, tb=tb,
                        c0A=c0A, db=db):
                    vector.wait_ge(C["dL"].sem, db)
                    for j, (c, w) in enumerate(achunks):
                        vector.wait_ge(C["t"].sem, tb + j + 1)
                        ins = vector.tensor_copy(ccs[:, c - c0A:c - c0A + w],
                                                 ps_b[:, 0:w])
                        C["v"].inc(ins)
                    nA = len(achunks)
                    boff = 2048 - c0A
                    for j, (c, w) in enumerate(bchunks):
                        vector.wait_ge(C["t"].sem, tb + nA + j + 1)
                        ins = vector.tensor_copy(
                            ccs[:, boff + c - 2049:boff + c - 2049 + w],
                            ps_b[:, 0:w])
                        C["v"].inc(ins)
                bodies(tensor=tbd, vector=vbd)
                bd_v = C["v"].n

                def dcc(sync, ti=ti, i0=i0, th=th, c0A=c0A, wB=wB, db=db,
                        bd_v=bd_v):
                    sync.wait_ge(C["v"].sem, bd_v)
                    sync.wait_ge(C["dL"].sem, db)
                    buf = cc_dram[th % 2]
                    boff = 2048 - c0A
                    C["dL"].inc(sync.dma_start(buf[:, c0A:2048], ccs[:, 0:boff]))
                    C["dL"].inc(sync.dma_start(buf[:, 2049:2049 + wB],
                                               ccs[:, boff:boff + wB]))
                    C["dL"].wait(sync)
                    src = bass.AP(buf.tensor, offset=(T - 1 - i0),
                                  ap=[[ROWP - 1, 128], [1, T]])
                    C["dL"].inc(sync.dma_start(winb[:], src))
                bodies(sync=dcc)
                win_d = C["dL"].n

                tb6 = C["t"].n
                s_pred = s_base + 2 * h

                def tac2(tensor, h=h, hp=hp, hr=hr, i0=i0, win_d=win_d,
                         s_pred=s_pred, tb6=tb6):
                    for half in range(2):
                        hof = half * 1024
                        tensor.wait_ge(C["s"].sem, s_pred + half)
                        for q2 in range(2):
                            ins = tensor.matmul(
                                ps_s[:, q2 * 512:(q2 + 1) * 512],
                                qubuf[hr:hr + 64, hp * T + i0:hp * T + i0 + 128],
                                kbuf[hr:hr + 64,
                                     hp * T + hof + q2 * 512:hp * T + hof + (q2 + 1) * 512],
                                start=True, stop=False)
                            C["t"].inc(ins)
                        if half == 0:
                            tensor.wait_ge(C["dL"].sem, win_d)
                        for q2 in range(2):
                            ins = tensor.matmul(
                                ps_s[:, q2 * 512:(q2 + 1) * 512], ident[:],
                                winb[:, hof + q2 * 512:hof + (q2 + 1) * 512],
                                start=False, stop=True)
                            C["t"].inc(ins)

                def sex2(scalar, h=h, tb6=tb6):
                    for half in range(2):
                        scalar.wait_ge(C["t"].sem, tb6 + 4 * (half + 1))
                        ins = scalar.activation(attnb[:, half * 1024:half * 1024 + 1024],
                                                ps_s[:], AF.Exp,
                                                bias=negB[:, h:h + 1],
                                                accum_out=Zb[:, half:half + 1])
                        C["s"].inc(ins)
                bodies(tensor=tac2)
                sex_defer.append(sex2)
                exp_s = s_base + 2 * h + 2

                def vz(vector, exp_s=exp_s):
                    vector.wait_ge(C["s"].sem, exp_s)
                    ins = vector.tensor_tensor(Zb[:, 2:3], Zb[:, 0:1], Zb[:, 1:2],
                                               OP.add)
                    C["g"].inc(ins)
                    vector.wait_ge(C["g"].sem, C["g"].n)
                    ins = vector.reciprocal(Zb[:, 3:4], Zb[:, 2:3])
                    C["g"].inc(ins)
                    vector.wait_ge(C["g"].sem, C["g"].n)
                    ins = vector.tensor_copy(zib[:], Zb[:, 3:4])
                    C["v"].inc(ins)
                bodies(vector=vz)
                z_v = C["v"].n
                if _DBG_L1A and ti == 0 and h == 0:
                    yo = ctx["ydbg"]

                    def ddmp(gp, z_v=z_v):
                        gp.wait_ge(C["v"].sem, z_v)
                        C["dO"].inc(gp.dma_start(yo[1, 0:128, :], attnb[:]))
                        C["dO"].inc(gp.dma_start(yo[1, 128:256, :], winb[:]))
                        C["dO"].inc(gp.dma_start(yo[1, 256:384, 0:16], negB[:]))
                        C["dO"].inc(gp.dma_start(yo[1, 384:512, 0:4], Zb[:]))
                        C["dO"].inc(gp.dma_start(yo[1, 512:640, :], pbuf[:, 0:T]))
                        C["dO"].inc(gp.dma_start(yo[1, 640:768, :], qubuf[:, 0:T]))
                        C["dO"].inc(gp.dma_start(yo[1, 768:896, :], kbuf[:, 0:T]))
                    bodies(gpsimd=ddmp)

                def gdiag(gpsimd, z_v=z_v):
                    gpsimd.wait_ge(C["v"].sem, z_v)
                    ins = gpsimd.affine_select(
                        diagt[:], zib[:].broadcast_to((128, 128)),
                        pattern=[[-1, 128]], compare_op=OP.is_equal, fill=0.0,
                        base=0, channel_multiplier=1)
                    C["g"].inc(ins)
                bodies(gpsimd=gdiag)
                g_g = C["g"].n

                for half in range(2):
                    hof = half * 1024
                    tb7 = C["t"].n
                    vb7 = C["v"].n

                    def ttr(tensor, hof=hof, g_g=g_g, vb7=vb7):
                        tensor.wait_ge(C["g"].sem, g_g)
                        tensor.wait_ge(C["v"].sem, vb7)
                        for jc in range(8):
                            ins = tensor.matmul(
                                ps_T[:, jc * 128:(jc + 1) * 128],
                                attnb[:, hof + jc * 128:hof + (jc + 1) * 128],
                                diagt[:], start=True, stop=True)
                            C["t"].inc(ins)

                    def vtr(vector, hof=hof, tb7=tb7):
                        vector.wait_ge(C["t"].sem, tb7 + 8)
                        ins = vector.tensor_copy(attnT[:, hof:hof + 1024], ps_T[:])
                        C["v"].inc(ins)
                    bodies(tensor=ttr, vector=vtr)
                tr_v = C["v"].n

                def tpv(tensor, h=h, hr=hr, hp=hp, tr_v=tr_v):
                    tensor.wait_ge(C["v"].sem, tr_v)
                    for jc in range(16):
                        ins = tensor.matmul(
                            ps_pv[hr:hr + 64, hp * 128:(hp + 1) * 128],
                            vtbuf[:, jc * D + h * 64:jc * D + (h + 1) * 64],
                            attnT[:, jc * 128:(jc + 1) * 128],
                            start=(jc == 0), stop=(jc == 15))
                        C["t"].inc(ins)
                bodies(tensor=tpv)

            for _sx in sex_defer:
                bodies(scalar=_sx)
            assert C["s"].n == s_base + 32

            pv_t = C["t"].n
            vb8 = C["v"].n

            def vpv(vector, pv_t=pv_t):
                vector.wait_ge(C["t"].sem, pv_t)
                ins = vector.tensor_copy(aof[:], ps_pv[:])
                C["v"].inc(ins)
            bodies(vector=vpv)
            if _DBG_L1A and ti == 0:
                yo3 = ctx["ydbg"]
                aov = C["v"].n

                def dao(gp, aov=aov):
                    gp.wait_ge(C["v"].sem, aov)
                    C["dO"].inc(gp.dma_start(yo3[1, 0:128, 0:1024], aof[:]))
                bodies(gpsimd=dao)
            ao_v = C["v"].n
            tb8 = C["t"].n

            def twf(tensor, ao_v=ao_v):
                tensor.wait_ge(C["v"].sem, ao_v)
                for oc in range(NCK):
                    for kc in range(NCK):
                        ins = tensor.matmul(
                            ps_s[:, oc * 128:(oc + 1) * 128],
                            wf[:, kc * D + oc * 128:kc * D + (oc + 1) * 128],
                            aof[:, kc * 128:(kc + 1) * 128],
                            start=(kc == 0), stop=(kc == NCK - 1))
                        C["t"].inc(ins)
            bodies(tensor=twf)
            wf_t = C["t"].n
            vb9 = C["v"].n
            sb9 = C["s"].n

            def vrs(vector, ti=ti, i0=i0, wf_t=wf_t):
                vector.wait_ge(C["t"].sem, wf_t)
                for oc in range(NCK):
                    xsl = xbuf[:, xcol(oc, i0):xcol(oc, i0 + 128)]
                    ins = vector.scalar_tensor_tensor(
                        xsl, ps_s[:, oc * 128:(oc + 1) * 128],
                        ctx["vec"]["bf"][:, oc:oc + 1], xsl, OP.add, OP.add)
                    C["v"].inc(ins)

            def srs(scalar, ti=ti, i0=i0, vb9=vb9, tb_=None):
                for oc in range(NCK):
                    scalar.wait_ge(C["v"].sem, vb9 + oc + 1)
                    ins = scalar.activation(sqv[:, (oc % 2) * 128:(oc % 2 + 1) * 128],
                                            xbuf[:, xcol(oc, i0):xcol(oc, i0 + 128)],
                                            AF.Square)
                    C["s"].inc(ins)

            def tst(tensor, ti=ti, i0=i0, vb9=vb9, sb9=sb9):
                for oc in range(NCK):
                    tensor.wait_ge(C["v"].sem, vb9 + oc + 1)
                    ins = tensor.matmul(pstt[0:1, 0:128], onescl[:],
                                        xbuf[:, xcol(oc, i0):xcol(oc, i0 + 128)],
                                        start=(oc == 0), stop=(oc == NCK - 1))
                    C["t"].inc(ins)
                    tensor.wait_ge(C["s"].sem, sb9 + oc + 1)
                    ins = tensor.matmul(pstt[0:1, 128:256], onescl[:],
                                        sqv[:, (oc % 2) * 128:(oc % 2 + 1) * 128],
                                        start=(oc == 0), stop=(oc == NCK - 1))
                    C["t"].inc(ins)
            bodies(vector=vrs, scalar=srs, tensor=tst)
            st_t = C["t"].n

            def vsl(vector, st_t=st_t, dl0=C["dL"].n):
                vector.wait_ge(C["t"].sem, st_t)
                vector.wait_ge(C["dL"].sem, dl0)
                vector.tensor_copy(lnslot[0:1, 0:128], pstt[0:1, 0:128])
                ins = vector.tensor_copy(lnslot[0:1, 128:256], pstt[0:1, 128:256])
                C["v"].inc(ins)
            bodies(vector=vsl)
            sl_v = C["v"].n

            def dsl(sync, ti=ti, i0=i0, sl_v=sl_v):
                sync.wait_ge(C["v"].sem, sl_v)
                C["dL"].inc(sync.dma_start(lnstats[0:1, i0:i0 + 128],
                                           lnslot[0:1, 0:128]))
                C["dL"].inc(sync.dma_start(lnstats[1:2, i0:i0 + 128],
                                           lnslot[0:1, 128:256]))
            bodies(sync=dsl)
            if _DBG_L1A and ti == 0:
                yo4 = ctx["ydbg"]
                prv = C["v"].n

                def dpre(gp, prv=prv, i0=i0):
                    gp.wait_ge(C["v"].sem, prv)
                    for oc in range(NCK):
                        C["dO"].inc(gp.dma_start(
                            yo4[1, 640 + oc * 16:640 + oc * 16 + 16, 0:128],
                            xbuf[0:16, ctx["xcol"](oc, i0):ctx["xcol"](oc, i0 + 128)]))
                bodies(gpsimd=dpre)


PHASE_FNS = {"l0a": _l0a, "l0f": _l0f, "l1a": _l1a, "l1f": _l1f}


def _l0a_stage(ctx, g):
    nc, bodies, C, xcol = ctx["nc"], ctx["bodies"], ctx["C"], ctx["xcol"]
    vec, dvec, xbuf = ctx["vec"], ctx["dvec"], ctx["xbuf"]
    t0 = g["t0"]
    mix, dxs, xxs, fscr = g["mix"], g["dxs"], g["xxs"], g["fscr"]
    rbf, kbf, vT, ew = g["rbf"], g["kbf"], g["vT"], g["ew"]
    t5f, w1, w2 = g["t5f"], g["w1"], g["w2"]
    wr, wk, wv, dw1, dw2 = g["wr"], g["wk"], g["wv"], g["dw1"], g["dw2"]
    d1b = ctx["stage"][0:64, 512:768].bitcast(BF16)

    def vdx(vector):
        for oc in range(NCK):
            def sl(d):
                return xbuf[:, xcol(oc, t0) + d:xcol(oc, t0 + SC) + d]
            vector.tensor_scalar(fscr[:], sl(-1), dvec["tap0"][:, oc:oc + 1],
                                 None, OP.mult)
            vector.scalar_tensor_tensor(fscr[:], sl(0), dvec["tap1m1"][:, oc:oc + 1],
                                        fscr[:], OP.mult, OP.add)
            ins = vector.scalar_tensor_tensor(
                dxs[:, oc * SC:(oc + 1) * SC], sl(1), dvec["tap2"][:, oc:oc + 1],
                fscr[:], OP.mult, OP.add)
            C["v"].inc(ins)
            ins = vector.scalar_tensor_tensor(
                xxs[:, oc * SC:(oc + 1) * SC], dxs[:, oc * SC:(oc + 1) * SC],
                vec["maa_x"][:, oc:oc + 1], sl(0), OP.mult, OP.add)
            C["v"].inc(ins)
    bodies(vector=vdx)
    xx_ready = C["v"].n

    with ExitStack() as ps:
        Q = ps.enter_context
        pt5 = [Q(nc.psum_tensor(_nm(f"pt5{i}"), [32, 512], F32)) for i in range(4)]
        tb = C["t"].n

        def t5mm(tensor):
            tensor.wait_ge(C["v"].sem, xx_ready)
            for f in range(4):
                for kc in range(NCK):
                    ins = tensor.matmul(
                        pt5[f][:], w1[:, kc * 160 + f * 32:kc * 160 + (f + 1) * 32],
                        xxs[:, kc * SC:(kc + 1) * SC],
                        start=(kc == 0), stop=(kc == NCK - 1))
                    C["t"].inc(ins)

        def t5act(scalar):
            for f in range(4):
                scalar.wait_ge(C["t"].sem, tb + 8 * (f + 1))
                ins = scalar.activation(t5f[0:32, f * SC:(f + 1) * SC],
                                        pt5[f][:], AF.Tanh)
                C["s"].inc(ins)
        bodies(tensor=t5mm, scalar=t5act)
        t5done = C["s"].n

    with ExitStack() as ps:
        Q = ps.enter_context
        pm = [Q(nc.psum_tensor(_nm(f"pmf{i}"), [128, 512], F32)) for i in range(2)]
        pp = [Q(nc.psum_tensor(_nm(f"ppf{i}"), [128, 512], F32)) for i in range(2)]
        pd1 = Q(nc.psum_tensor(_nm("pd1"), [64, 512], F32))

        for f, vnm, act in ((1, "maa_k", "k"), (2, "maa_v", "v"),
                            (3, "maa_r", "r"), (0, "maa_w", "w")):
            tbm = C["t"].n
            vbm = C["v"].n

            def mf_mm(tensor, f=f, tbm=tbm, vbm=vbm):
                tensor.wait_ge(C["s"].sem, t5done)
                for oc in range(NCK):
                    if oc >= 2:
                        tensor.wait_ge(C["v"].sem, vbm + oc - 1)
                    ins = tensor.matmul(
                        pm[oc % 2][:], w2[0:32, f * D + oc * 128:f * D + (oc + 1) * 128],
                        t5f[0:32, f * SC:(f + 1) * SC], start=True, stop=True)
                    C["t"].inc(ins)

            def mf_mix(vector, vnm=vnm, tbm=tbm):
                for oc in range(NCK):
                    vector.wait_ge(C["t"].sem, tbm + oc + 1)
                    vector.scalar_tensor_tensor(
                        fscr[:], pm[oc % 2][:], vec[vnm][:, oc:oc + 1],
                        dxs[:, oc * SC:(oc + 1) * SC], OP.add, OP.mult)
                    ins = vector.tensor_tensor(
                        mix[:, oc * SC:(oc + 1) * SC], fscr[:],
                        xbuf[:, xcol(oc, t0):xcol(oc, t0 + SC)], OP.add)
                    C["v"].inc(ins)
            bodies(tensor=mf_mm, vector=mf_mix)
            mix_ready = C["v"].n

            if act in ("k", "r"):
                dst = kbf if act == "k" else rbf
                W_ = wk if act == "k" else wr
                tbp = C["t"].n
                vbp = C["v"].n

                def proj(tensor, W_=W_, mix_ready=mix_ready, vbp=vbp):
                    tensor.wait_ge(C["v"].sem, mix_ready)
                    for oc in range(NCK):
                        if oc >= 2:
                            tensor.wait_ge(C["v"].sem, vbp + oc - 1)
                        for kc in range(NCK):
                            ins = tensor.matmul(
                                pp[oc % 2][:],
                                W_[:, kc * D + oc * 128:kc * D + (oc + 1) * 128],
                                mix[:, kc * SC:(kc + 1) * SC],
                                start=(kc == 0), stop=(kc == NCK - 1))
                            C["t"].inc(ins)

                def evac(vector, dst=dst, tbp=tbp):
                    for oc in range(NCK):
                        vector.wait_ge(C["t"].sem, tbp + 8 * (oc + 1))
                        ins = vector.tensor_copy(dst[:, oc * SC:(oc + 1) * SC],
                                                 pp[oc % 2][:])
                        C["v"].inc(ins)
                bodies(tensor=proj, vector=evac)
            elif act == "v":
                tbp = C["t"].n
                vbp = C["v"].n

                def projv(tensor, mix_ready=mix_ready, vbp=vbp):
                    tensor.wait_ge(C["v"].sem, mix_ready)
                    for tb_ in range(4):
                        for half in range(2):
                            j = (tb_ * 2 + half) % 2
                            if tb_ * 2 + half >= 2:
                                tensor.wait_ge(C["v"].sem, vbp + tb_ * 2 + half - 1)
                            for kc in range(NCK):
                                ins = tensor.matmul(
                                    pp[j][:],
                                    mix[:, kc * SC + tb_ * 128:kc * SC + (tb_ + 1) * 128],
                                    wv[:, kc * D + half * 512:kc * D + (half + 1) * 512],
                                    start=(kc == 0), stop=(kc == NCK - 1))
                                C["t"].inc(ins)

                def evacv(vector, tbp=tbp):
                    for tb_ in range(4):
                        for half in range(2):
                            j = (tb_ * 2 + half) % 2
                            vector.wait_ge(C["t"].sem, tbp + 8 * (tb_ * 2 + half + 1))
                            ins = vector.tensor_copy(
                                vT[:, tb_ * D + half * 512:tb_ * D + (half + 1) * 512],
                                pp[j][:])
                            C["v"].inc(ins)
                bodies(tensor=projv, vector=evacv)
            else:
                tbp = C["t"].n
                sbp = C["s"].n

                def projw(tensor, mix_ready=mix_ready, sbp=sbp):
                    tensor.wait_ge(C["v"].sem, mix_ready)
                    for kc in range(NCK):
                        ins = tensor.matmul(
                            pd1[:], dw1[:, kc * 64:(kc + 1) * 64],
                            mix[:, kc * SC:(kc + 1) * SC],
                            start=(kc == 0), stop=(kc == NCK - 1))
                        C["t"].inc(ins)
                    for oc in range(NCK):
                        tensor.wait_ge(C["s"].sem, sbp + 1)
                        if oc >= 2:
                            tensor.wait_ge(C["s"].sem, sbp + oc)
                        ins = tensor.matmul(
                            pp[oc % 2][:], dw2[0:64, oc * 128:(oc + 1) * 128],
                            d1b, start=True, stop=True)
                        C["t"].inc(ins)

                def actw(scalar, tbp=tbp):
                    scalar.wait_ge(C["t"].sem, tbp + NCK)
                    ins = scalar.activation(d1b, pd1[:], AF.Tanh)
                    C["s"].inc(ins)
                    for oc in range(NCK):
                        scalar.wait_ge(C["t"].sem, tbp + NCK + oc + 1)
                        ins = scalar.activation(
                            ew[:, oc * SC:(oc + 1) * SC], pp[oc % 2][:], AF.Exp,
                            bias=vec["time_decay"][:, oc:oc + 1])
                        C["s"].inc(ins)
                bodies(tensor=projw, scalar=actw)


def _l0a_scan(ctx, g, sp):
    nc, bodies, C = ctx["nc"], ctx["bodies"], ctx["C"]
    maskf, maskb, ident = ctx["maskf"], ctx["maskb"], ctx["ident"]
    pas, t0, cc_ = g["pas"], g["t0"], g["cc_"]
    ybuf, rbf, kbf, vT, ew = g["ybuf"], g["rbf"], g["kbf"], g["vT"], g["ew"]
    ones128, S0 = g["ones128"], g["S0"]
    Ab, Nb, eA, eAm, eMid, eL = g["Ab"], g["Nb"], g["eA"], g["eAm"], g["eMid"], g["eL"]
    rt, kt, scbA, scbB, ktT = g["rt"], g["kt"], g["scbA"], g["scbB"], g["ktT"]
    s0f, s0b, sscr = g["s0f"], g["s0b"], g["sscr"]
    psc0, psc1, pkt, py, psS = sp["psc0"], sp["psc1"], sp["pkt"], sp["py"], sp["psS"]
    mask = maskf if pas == "A" else maskb
    lo = cc_ * 128

    for oc in range(NCK):
        i = sp["step"] % 2
        st = sp["step"]
        vb0, sb0, tb0 = sp["vb0"], sp["sb0"], sp["tb0"]
        ewsl = ew[:, oc * SC + lo:oc * SC + lo + 128]
        rsl = rbf[:, oc * SC + lo:oc * SC + lo + 128]
        ksl = kbf[:, oc * SC + lo:oc * SC + lo + 128]
        vTa = vT[:, cc_ * D + oc * 128:cc_ * D + oc * 128 + 64]
        vTb = vT[:, cc_ * D + oc * 128 + 64:cc_ * D + (oc + 1) * 128]
        ysl = ybuf[:, oc * T + t0 + lo:oc * T + t0 + lo + 128]
        s0sl = S0[:, oc * 64:(oc + 1) * 64]

        def V1(vector, i=i, st=st, ewsl=ewsl):
            if st >= 2:
                vector.wait_ge(C["s"].sem, sb0 + st - 1)
            vector.tensor_tensor_scan(Ab[i][:], ones128[:], ewsl, 0.0,
                                      OP.mult, OP.subtract)
            ins = vector.tensor_scalar(Nb[i][:], Ab[i][:], Ab[i][:, 63:64],
                                       None, OP.subtract)
            C["v"].inc(ins)

        def S1(scalar, i=i, st=st):
            scalar.wait_ge(C["v"].sem, vb0 + 4 * st + 1)
            scalar.activation(eA[i][:], Nb[i][:], AF.Exp)
            scalar.activation(eAm[i][:], Nb[i][:], AF.Exp, scale=-1.0)
            scalar.activation(eMid[i][:], Ab[i][:, 63:64], AF.Exp)
            ins = scalar.activation(eL[i][:], Nb[i][:, 127:128], AF.Exp)
            C["s"].inc(ins)

        def V2(vector, i=i, st=st, rsl=rsl, ksl=ksl, s0sl=s0sl):
            vector.wait_ge(C["s"].sem, sb0 + st + 1)
            if st >= 2:
                vector.wait_ge(C["t"].sem, tb0 + 2 * (st - 2) + 2)
            if pas == "A":
                vector.tensor_tensor(rt[i][:], rsl, eA[i][:], OP.mult)
                vector.tensor_tensor(kt[i][:], ksl, eAm[i][:], OP.mult)
                ins = vector.tensor_scalar(s0f[i][:], s0sl, eMid[i][:], None, OP.mult)
            else:
                vector.tensor_tensor(rt[i][:], rsl, eAm[i][:], OP.mult)
                vector.tensor_tensor(kt[i][:], ksl, eA[i][:], OP.mult)
                ins = vector.tensor_scalar(s0f[i][:], s0sl, eL[i][:], None, OP.mult)
            C["g"].inc(ins)
            vector.wait_ge(C["g"].sem, C["g"].n)
            ins = vector.tensor_copy(s0b[i][:], s0f[i][:])
            C["v"].inc(ins)

        def T1(tensor, i=i, st=st):
            tensor.wait_ge(C["v"].sem, vb0 + 4 * st + 2)
            tensor.matmul(psc0[:, 0:128], kt[i][0:64, :], rt[i][0:64, :],
                          start=True, stop=True)
            tensor.matmul(psc1[:, 0:128], kt[i][64:128, :], rt[i][64:128, :],
                          start=True, stop=True)
            ins = tensor.matmul(pkt[:, 0:128], kt[i][:], ident[:],
                                start=True, stop=True)
            C["t"].inc(ins)

        def V3(vector, i=i, st=st):
            vector.wait_ge(C["t"].sem, tb0 + 2 * st + 1)
            vector.tensor_tensor(scbA[i][:], psc0[:, 0:128], mask[:], OP.mult)
            vector.tensor_tensor(scbB[i][:], psc1[:, 0:128], mask[:], OP.mult)
            ins = vector.tensor_copy(ktT[i][:], pkt[:, 0:128])
            C["v"].inc(ins)

        def T2(tensor, i=i, st=st, vTa=vTa, vTb=vTb):
            tensor.wait_ge(C["v"].sem, vb0 + 4 * st + 3)
            tensor.matmul(py[0:64, 0:128], vTa, scbA[i][:], start=True, stop=False)
            tensor.matmul(py[64:128, 0:128], vTb, scbB[i][:], start=True, stop=False)
            tensor.matmul(py[0:64, 0:128], s0b[i][0:64, :], rt[i][0:64, :],
                          start=False, stop=True)
            tensor.matmul(py[64:128, 0:128], s0b[i][64:128, :], rt[i][64:128, :],
                          start=False, stop=True)
            tensor.matmul(psS[0:64, 0:64], ktT[i][:, 0:64], vTa, start=True, stop=True)
            ins = tensor.matmul(psS[64:128, 0:64], ktT[i][:, 64:128], vTb,
                                start=True, stop=True)
            C["t"].inc(ins)

        def V4(vector, i=i, st=st, ysl=ysl, s0sl=s0sl):
            vector.wait_ge(C["t"].sem, tb0 + 2 * st + 2)
            if pas == "A":
                vector.tensor_copy(ysl, py[:, 0:128])
            else:
                vector.tensor_tensor(ysl, ysl, py[:, 0:128], OP.add)
            ins = vector.tensor_tensor(sscr[i][:], psS[:, 0:64], s0f[i][:], OP.add)
            C["g"].inc(ins)
            vector.wait_ge(C["g"].sem, C["g"].n)
            esel = eL[i] if pas == "A" else eMid[i]
            ins = vector.tensor_scalar(s0sl, sscr[i][:], esel[:], None, OP.mult)
            C["v"].inc(ins)

        bodies(vector=V1)
        sp["s1defer"].append(S1)
        bodies(vector=V2)
        bodies(tensor=T1)
        bodies(vector=V3)
        bodies(tensor=T2)
        bodies(vector=V4)
        sp["step"] += 1


def _l0a_passc(ctx, g):
    nc, bodies, C, xcol = ctx["nc"], ctx["bodies"], ctx["C"], ctx["xcol"]
    vec, dvec, xbuf = ctx["vec"], ctx["dvec"], ctx["xbuf"]
    t0, sc = g["t0"], g["sc"]
    mix, dxs, xxs, fscr = g["mix"], g["dxs"], g["xxs"], g["fscr"]
    ybuf, w1, w2 = g["ybuf"], g["w1"], g["w2"]
    wg_, wgate, wo = g["wg_"], g["wgate"], g["wo"]
    gnselb, gnbcb, onesc = g["gnselb"], g["gnbcb"], g["onesc"]
    lnslot, lnstats = g["lnslot"], g["lnstats"]
    eps_gn = 1e-5 * 64.0
    gsilu = xxs

    bstash = g["bstash"]
    with ExitStack() as s1:
        Q = s1.enter_context
        tgs = [Q(nc.sbuf_tensor(_nm(f"tgs{i}"), [128, SC], BF16)) for i in range(2)]
        t5g = Q(nc.sbuf_tensor(_nm("t5g"), [32, SC], BF16))
        pt5g = Q(nc.psum_tensor(_nm("pt5g"), [32, 512], F32))
        sq2 = [Q(nc.sbuf_tensor(_nm(f"sq2c{i}"), [128, SC], BF16)) for i in range(2)]
        mvt = Q(nc.sbuf_tensor(_nm("mvt"), [16, 3 * SC], F32))
        mvb = Q(nc.sbuf_tensor(_nm("mvb"), [16, 2 * SC], BF16))
        pm = [Q(nc.psum_tensor(_nm(f"pmc{i}"), [128, 512], F32)) for i in range(2)]
        pp = [Q(nc.psum_tensor(_nm(f"ppc{i}"), [128, 512], F32)) for i in range(2)]
        pst = Q(nc.psum_tensor(_nm("pstc"), [16, 512], F32))
        psq = Q(nc.psum_tensor(_nm("psqc"), [16, 512], F32))

        def vdx(vector):
            for oc in range(NCK):
                def sl(d):
                    return xbuf[:, xcol(oc, t0) + d:xcol(oc, t0 + SC) + d]
                vector.tensor_scalar(fscr[:], sl(-1), dvec["tap0"][:, oc:oc + 1],
                                     None, OP.mult)
                vector.scalar_tensor_tensor(fscr[:], sl(0), dvec["tap1m1"][:, oc:oc + 1],
                                            fscr[:], OP.mult, OP.add)
                ins = vector.scalar_tensor_tensor(
                    dxs[:, oc * SC:(oc + 1) * SC], sl(1), dvec["tap2"][:, oc:oc + 1],
                    fscr[:], OP.mult, OP.add)
                C["v"].inc(ins)
            if sc > 0:
                for oc in range(NCK):
                    c0 = xcol(oc, t0)
                    ins = vector.tensor_scalar(
                        fscr[:, 0:1], bstash[:, oc:oc + 1],
                        dvec["tap0"][:, oc:oc + 1], None, OP.mult)
                    C["g"].inc(ins)
                    vector.wait_ge(C["g"].sem, C["g"].n)
                    ins = vector.scalar_tensor_tensor(
                        fscr[:, 0:1], xbuf[:, c0:c0 + 1],
                        dvec["tap1m1"][:, oc:oc + 1], fscr[:, 0:1], OP.mult, OP.add)
                    C["g"].inc(ins)
                    vector.wait_ge(C["g"].sem, C["g"].n)
                    ins = vector.scalar_tensor_tensor(
                        dxs[:, oc * SC:oc * SC + 1], xbuf[:, c0 + 1:c0 + 2],
                        dvec["tap2"][:, oc:oc + 1], fscr[:, 0:1], OP.mult, OP.add)
                    C["v"].inc(ins)
            if sc < NSC - 1:
                for oc in range(NCK):
                    ins = vector.tensor_copy(bstash[:, oc:oc + 1],
                                             xbuf[:, xcol(oc, t0 + SC - 1):
                                                  xcol(oc, t0 + SC)])
                    C["v"].inc(ins)
        bodies(vector=vdx)
        dx_ready = C["v"].n

        def vxx(vector):
            for oc in range(NCK):
                ins = vector.scalar_tensor_tensor(
                    mix[:, oc * SC:(oc + 1) * SC], dxs[:, oc * SC:(oc + 1) * SC],
                    vec["maa_x"][:, oc:oc + 1],
                    xbuf[:, xcol(oc, t0):xcol(oc, t0 + SC)], OP.mult, OP.add)
                C["v"].inc(ins)
        bodies(vector=vxx)
        xx_ready = C["v"].n
        tb5 = C["t"].n

        def t5gm(tensor):
            tensor.wait_ge(C["v"].sem, xx_ready)
            for kc in range(NCK):
                ins = tensor.matmul(
                    pt5g[:], w1[:, kc * 160 + 128:kc * 160 + 160],
                    mix[:, kc * SC:(kc + 1) * SC],
                    start=(kc == 0), stop=(kc == NCK - 1))
                C["t"].inc(ins)

        def t5ga(scalar):
            scalar.wait_ge(C["t"].sem, tb5 + NCK)
            ins = scalar.activation(t5g[:], pt5g[:], AF.Tanh)
            C["s"].inc(ins)
        bodies(tensor=t5gm, scalar=t5ga)
        t5g_done = C["s"].n

        tbm = C["t"].n
        vbm = C["v"].n

        def m4mm(tensor):
            tensor.wait_ge(C["s"].sem, t5g_done)
            for oc in range(NCK):
                if oc >= 2:
                    tensor.wait_ge(C["v"].sem, vbm + oc - 1)
                ins = tensor.matmul(
                    pm[oc % 2][:], w2[0:32, 4 * D + oc * 128:4 * D + (oc + 1) * 128],
                    t5g[:], start=True, stop=True)
                C["t"].inc(ins)

        def m4mix(vector):
            for oc in range(NCK):
                vector.wait_ge(C["t"].sem, tbm + oc + 1)
                vector.scalar_tensor_tensor(
                    fscr[:], pm[oc % 2][:], vec["maa_g"][:, oc:oc + 1],
                    dxs[:, oc * SC:(oc + 1) * SC], OP.add, OP.mult)
                ins = vector.tensor_tensor(
                    mix[:, oc * SC:(oc + 1) * SC], fscr[:],
                    xbuf[:, xcol(oc, t0):xcol(oc, t0 + SC)], OP.add)
                C["v"].inc(ins)
        bodies(tensor=m4mm, vector=m4mix)
        mix_ready = C["v"].n

        tbg = C["t"].n
        sbg = C["s"].n
        vbg = C["v"].n

        def gmm(tensor):
            tensor.wait_ge(C["v"].sem, mix_ready)
            for oc in range(NCK):
                if oc >= 2:
                    tensor.wait_ge(C["v"].sem, vbg + oc - 1)
                for kc in range(NCK):
                    ins = tensor.matmul(
                        pp[oc % 2][:], wg_[:, kc * D + oc * 128:kc * D + (oc + 1) * 128],
                        mix[:, kc * SC:(kc + 1) * SC],
                        start=(kc == 0), stop=(kc == NCK - 1))
                    C["t"].inc(ins)

        def gact(scalar):
            for oc in range(NCK):
                scalar.wait_ge(C["t"].sem, tbg + 8 * (oc + 1))
                if oc >= 2:
                    scalar.wait_ge(C["v"].sem, vbg + oc - 1)
                ins = scalar.activation(tgs[oc % 2][:], pp[oc % 2][:], AF.Tanh)
                C["s"].inc(ins)

        def gsil(vector):
            for oc in range(NCK):
                vector.wait_ge(C["s"].sem, sbg + oc + 1)
                ins = vector.scalar_tensor_tensor(
                    gsilu[:, oc * SC:(oc + 1) * SC], tgs[oc % 2][:], 1.0,
                    pp[oc % 2][:], OP.add, OP.mult)
                C["v"].inc(ins)
        bodies(tensor=gmm, scalar=gact, vector=gsil)

        tbt = C["t"].n
        sbt = C["s"].n
        vbt = C["v"].n

        def tmm(tensor):
            tensor.wait_ge(C["v"].sem, vbt)
            for oc in range(NCK):
                if oc >= 2:
                    tensor.wait_ge(C["v"].sem, vbt + oc - 1)
                for kc in range(NCK):
                    ins = tensor.matmul(
                        pp[oc % 2][:], wgate[:, kc * D + oc * 128:kc * D + (oc + 1) * 128],
                        dxs[:, kc * SC:(kc + 1) * SC],
                        start=(kc == 0), stop=(kc == NCK - 1))
                    C["t"].inc(ins)

        def tact(scalar):
            for oc in range(NCK):
                scalar.wait_ge(C["t"].sem, tbt + 8 * (oc + 1))
                if oc >= 2:
                    scalar.wait_ge(C["v"].sem, vbt + oc - 1)
                ins = scalar.activation(tgs[oc % 2][:], pp[oc % 2][:], AF.Tanh,
                                        bias=dvec["gbh"][:, oc:oc + 1])
                C["s"].inc(ins)

        def tapply(vector):
            for oc in range(NCK):
                vector.wait_ge(C["s"].sem, sbt + oc + 1)
                ysl = ybuf[:, oc * T + t0:oc * T + t0 + SC]
                ins = vector.scalar_tensor_tensor(
                    ysl, tgs[oc % 2][:], 1.0, ysl, OP.add, OP.mult)
                C["v"].inc(ins)
        bodies(tensor=tmm, scalar=tact, vector=tapply)
        y_ready = C["v"].n
        sbq = C["s"].n
        tbq = C["t"].n

        def sqs_(scalar):
            for oc in range(NCK):
                scalar.wait_ge(C["v"].sem, y_ready - NCK + oc + 1)
                if oc >= 2:
                    scalar.wait_ge(C["t"].sem, tbq + 2 * (oc - 1))
                ins = scalar.activation(sq2[oc % 2][:],
                                        ybuf[:, oc * T + t0:oc * T + t0 + SC],
                                        AF.Square)
                C["s"].inc(ins)

        def gnmm(tensor):
            tensor.wait_ge(C["dL"].sem, g["dl_gn"])
            for oc in range(NCK):
                tensor.wait_ge(C["v"].sem, y_ready - NCK + oc + 1)
                ins = tensor.matmul(pst[:], gnselb[:, oc * 16:(oc + 1) * 16],
                                    ybuf[:, oc * T + t0:oc * T + t0 + SC],
                                    start=(oc == 0), stop=(oc == NCK - 1))
                C["t"].inc(ins)
                tensor.wait_ge(C["s"].sem, sbq + oc + 1)
                ins = tensor.matmul(psq[:], gnselb[:, oc * 16:(oc + 1) * 16],
                                    sq2[oc % 2][:],
                                    start=(oc == 0), stop=(oc == NCK - 1))
                C["t"].inc(ins)
        bodies(scalar=sqs_, tensor=gnmm)
        st_done = C["t"].n

        def vstat(vector):
            vector.wait_ge(C["t"].sem, st_done)
            vector.tensor_scalar(mvt[0:16, 0:SC], pst[:], 1.0 / 64.0, None, OP.mult)
            vector.tensor_scalar(mvt[0:16, SC:2 * SC], psq[:], 1.0 / 64.0, None, OP.mult)
            vector.tensor_tensor(mvt[0:16, 2 * SC:3 * SC], mvt[0:16, 0:SC],
                                 mvt[0:16, 0:SC], OP.mult)
            ins = vector.tensor_tensor(mvt[0:16, SC:2 * SC], mvt[0:16, SC:2 * SC],
                                       mvt[0:16, 2 * SC:3 * SC], OP.subtract)
            C["g"].inc(ins)
            vector.wait_ge(C["g"].sem, C["g"].n)
            ins = vector.tensor_scalar(mvt[0:16, SC:2 * SC], mvt[0:16, SC:2 * SC],
                                       4.0 * eps_gn, None, OP.add)
            C["v"].inc(ins)
        bodies(vector=vstat)

        def sstat(scalar):
            C["v"].wait(scalar)
            ins = scalar.activation(mvt[0:16, 2 * SC:3 * SC], mvt[0:16, SC:2 * SC],
                                    AF.Sqrt)
            C["s"].inc(ins)
        bodies(scalar=sstat)

        def vstat2(vector):
            C["s"].wait(vector)
            ins = vector.reciprocal(mvt[0:16, SC:2 * SC], mvt[0:16, 2 * SC:3 * SC])
            C["g"].inc(ins)
            vector.wait_ge(C["g"].sem, C["g"].n)
            vector.tensor_copy(mvb[0:16, 0:SC], mvt[0:16, 0:SC])
            ins = vector.tensor_copy(mvb[0:16, SC:2 * SC], mvt[0:16, SC:2 * SC])
            C["v"].inc(ins)
        bodies(vector=vstat2)
    ctx["barrier"]()

    with ExitStack() as s2:
        Q = s2.enter_context
        pmb = Q(nc.psum_tensor(_nm("pmbc"), [128, 512], F32))
        prb = Q(nc.psum_tensor(_nm("prbc"), [128, 512], F32))
        pwo = [Q(nc.psum_tensor(_nm(f"pwoc{i}"), [128, 512], F32)) for i in range(2)]
        pss2 = Q(nc.psum_tensor(_nm("pss2c"), [1, 512], F32))
        psq2 = Q(nc.psum_tensor(_nm("psq2c"), [1, 512], F32))
        sqv = Q(nc.sbuf_tensor(_nm("sqvc"), [128, SC], BF16))

        tb2 = C["t"].n
        vb2 = C["v"].n

        def bapply(tensor):
            tensor.wait_ge(C["dL"].sem, g["dl_gn"])
            for oc in range(NCK):
                if oc > 0:
                    tensor.wait_ge(C["v"].sem, vb2 + oc)
                ins = tensor.matmul(pmb[:], gnbcb[:, oc * 128:(oc + 1) * 128],
                                    mvb[0:16, 0:SC], start=True, stop=True)
                C["t"].inc(ins)
                ins = tensor.matmul(prb[:], gnbcb[:, oc * 128:(oc + 1) * 128],
                                    mvb[0:16, SC:2 * SC], start=True, stop=True)
                C["t"].inc(ins)

        def vapply(vector):
            for oc in range(NCK):
                vector.wait_ge(C["t"].sem, tb2 + 2 * (oc + 1))
                ysl = ybuf[:, oc * T + t0:oc * T + t0 + SC]
                vector.tensor_tensor(fscr[:], ysl, pmb[:], OP.subtract)
                vector.tensor_tensor(fscr[:], fscr[:], prb[:], OP.mult)
                vector.tensor_scalar(fscr[:], fscr[:], vec["ln_w"][:, oc:oc + 1],
                                     vec["ln_b"][:, oc:oc + 1], OP.mult, OP.add)
                ins = vector.tensor_tensor(mix[:, oc * SC:(oc + 1) * SC], fscr[:],
                                           gsilu[:, oc * SC:(oc + 1) * SC], OP.mult)
                C["v"].inc(ins)
        bodies(tensor=bapply, vector=vapply)
        yng_ready = C["v"].n

        tb3 = C["t"].n
        vb3 = C["v"].n
        sb3 = C["s"].n

        def womm(tensor):
            tensor.wait_ge(C["v"].sem, yng_ready)
            for oc in range(NCK):
                if oc >= 2:
                    tensor.wait_ge(C["v"].sem, vb3 + oc - 1)
                for kc in range(NCK):
                    ins = tensor.matmul(
                        pwo[oc % 2][:], wo[:, kc * D + oc * 128:kc * D + (oc + 1) * 128],
                        mix[:, kc * SC:(kc + 1) * SC],
                        start=(kc == 0), stop=(kc == NCK - 1))
                    C["t"].inc(ins)
            for oc in range(NCK):
                tensor.wait_ge(C["v"].sem, vb3 + oc + 1)
                ins = tensor.matmul(pss2[:], onesc[:],
                                    xbuf[:, xcol(oc, t0):xcol(oc, t0 + SC)],
                                    start=(oc == 0), stop=(oc == NCK - 1))
                C["t"].inc(ins)
                tensor.wait_ge(C["s"].sem, sb3 + oc + 1)
                ins = tensor.matmul(psq2[:], onesc[:], sqv[:],
                                    start=(oc == 0), stop=(oc == NCK - 1))
                C["t"].inc(ins)

        def vres(vector):
            for oc in range(NCK):
                vector.wait_ge(C["t"].sem, tb3 + 8 * (oc + 1))
                xsl = xbuf[:, xcol(oc, t0):xcol(oc, t0 + SC)]
                ins = vector.tensor_tensor(xsl, xsl, pwo[oc % 2][:], OP.add)
                C["v"].inc(ins)

        def sres(scalar):
            for oc in range(NCK):
                scalar.wait_ge(C["v"].sem, vb3 + oc + 1)
                if oc >= 1:
                    scalar.wait_ge(C["t"].sem, tb3 + 8 * NCK + 2 * oc)
                ins = scalar.activation(sqv[:], xbuf[:, xcol(oc, t0):xcol(oc, t0 + SC)],
                                        AF.Square)
                C["s"].inc(ins)
        bodies(tensor=womm, vector=vres, scalar=sres)
        fin = C["t"].n

        def vslot(vector, dl0=C["dL"].n):
            vector.wait_ge(C["t"].sem, fin)
            if sc > 0:
                vector.wait_ge(C["dL"].sem, dl0)
            vector.tensor_copy(lnslot[0:1, 0:SC], pss2[:])
            ins = vector.tensor_copy(lnslot[0:1, SC:2 * SC], psq2[:])
            C["v"].inc(ins)
        bodies(vector=vslot)
        vdone = C["v"].n

        def dstat(sync):
            sync.wait_ge(C["v"].sem, vdone)
            C["dL"].inc(sync.dma_start(lnstats[0:1, t0:t0 + SC], lnslot[0:1, 0:SC]))
            C["dL"].inc(sync.dma_start(lnstats[1:2, t0:t0 + SC], lnslot[0:1, SC:2 * SC]))
        bodies(sync=dstat)
